# version 16
# baseline (speedup 1.0000x reference)
import numpy as np
import ml_dtypes
from contextlib import ExitStack

import concourse.bass as bass
import concourse.mybir as mybir
from concourse.bass_utils import run_bass_kernel_spmd

F32 = mybir.dt.float32
BF16 = mybir.dt.bfloat16
AF = mybir.ActivationFunctionType
ALU = mybir.AluOpType
AX = mybir.AxisListType.X

Bb, Lb, H, DIN, NST, DTR, TW = 2, 8192, 1024, 2048, 16, 64, 4
NCORE = 8
TOK = 2048
CH = 512
NCHUNK = TOK // CH
NDT = DIN // 128
NKH = H // 128
NEXP, TOPK, NGRP = 32, 8, 4
RSCALE = 2.5


class Prog:
    ENGS = ("pe", "act", "dve", "pool", "sync", "gdma")
    NSLOT = 8

    def __init__(self):
        self.ops = []
        self.state = {}
        self.bar = set()
        self.dma_count = {"sync": 0, "gdma": 0}
        self.slot_last = {}
        self.phase = "setup"

    def emit(self, eng, fn, rd=(), wr=(), dma=False, inc=None, tag=""):
        idx = len(self.ops)
        op = dict(eng=eng, fn=fn, dma=dma, deps=set(), need_inc=False, inc=inc, tag=tag, phase=self.phase)
        if dma:
            slot = self.dma_count[eng] % self.NSLOT
            self.dma_count[eng] += 1
            op["sem_name"] = f"{eng}{slot}"
            op["need_inc"] = True
            prev = self.slot_last.get((eng, slot))
            if prev is not None:
                op["deps"].add(prev)
            self.slot_last[(eng, slot)] = idx
        else:
            op["sem_name"] = eng
        for b in self.bar:
            if self._xdep(b, eng, dma):
                op["deps"].add(b)
        for k in rd:
            st = self.state.get(k)
            if st is not None and st["w"] is not None and self._xdep(st["w"], eng, dma):
                op["deps"].add(st["w"])
        for k in wr:
            st = self.state.get(k)
            if st is not None:
                if st["w"] is not None and self._xdep(st["w"], eng, dma):
                    op["deps"].add(st["w"])
                for r in st["r"]:
                    if self._xdep(r, eng, dma):
                        op["deps"].add(r)
        self.ops.append(op)
        for k in rd:
            self.state.setdefault(k, {"w": None, "r": []})["r"].append(idx)
        for k in wr:
            self.state[k] = {"w": idx, "r": []}
        for d in op["deps"]:
            self.ops[d]["need_inc"] = True
        return idx

    def _xdep(self, prod_idx, eng, dma):
        p = self.ops[prod_idx]
        if p["dma"] or dma:
            return True
        if p["eng"] != eng:
            return True
        return eng != "pe"

    def barrier(self):
        last = {}
        for i, op in enumerate(self.ops):
            last[op["sem_name"]] = i
        self.bar = set(last.values())
        for i in self.bar:
            self.ops[i]["need_inc"] = True

    def materialize(self, sems):
        val = {s: 0 for s in sems}
        for op in self.ops:
            if op["need_inc"] or op["inc"] is not None:
                s = op["sem_name"]
                step = op["inc"] if op["inc"] is not None else (16 if op["dma"] else 1)
                val[s] += step
                op["sem_target"] = (s, val[s])
                op["step"] = step
            else:
                op["sem_target"] = None
        waited = {e: {} for e in self.ENGS}
        for op in self.ops:
            w = {}
            for d in op["deps"]:
                s, v = self.ops[d]["sem_target"]
                w[s] = max(w.get(s, 0), v)
            ww = []
            for s, v in w.items():
                if waited[op["eng"]].get(s, 0) < v:
                    waited[op["eng"]][s] = v
                    ww.append((s, v))
            op["waits"] = ww

        def run(eng_names, eng, nc=None):
            for op in self.ops:
                if op["eng"] not in eng_names:
                    continue
                if nc is not None:
                    with nc.named_scope(op["phase"]):
                        for s, v in op["waits"]:
                            eng.wait_ge(sems[s], v)
                        inst = op["fn"](eng)
                        if op["sem_target"] is not None:
                            inst.then_inc(sems[op["sem_name"]], op["step"])
                else:
                    for s, v in op["waits"]:
                        eng.wait_ge(sems[s], v)
                    inst = op["fn"](eng)
                    if op["sem_target"] is not None:
                        inst.then_inc(sems[op["sem_name"]], op["step"])
        return run


def make_program(do_cc=True):
    nc = bass.Bass("TRN2", target_bir_lowering=False)

    def din(name, shape, dt=F32):
        return nc.dram_tensor(name, shape, dt, kind="ExternalInput").ap()

    x_d = din("x_shard", [2 * 1027, H])
    mask_d = din("mask_in", [1, 1])
    posm_d = din("posmask", [8, 1])
    win_d = din("w_in_f", [H, 2 * DIN])
    binl_d = din("b_in_l", [128, 32])
    wconv_d = din("wconv_l", [128, NDT * TW])
    bconv_d = din("bconv_l", [128, NDT])
    wx_d = din("wx_f", [DIN, DTR + 2 * NST])
    wdt_d = din("wdt_f", [DTR, DIN])
    alog_d = din("alog_l", [128, NDT * NST])
    dl_d = din("d_l", [128, NDT])
    wout_d = din("wout_f", [DIN, H])
    gwT_d = din("gwT_f", [H, NEXP])
    gb_d = din("gb_row", [1, NEXP])
    egT_d = din("egT_f", [NEXP, H, 512])
    euT_d = din("euT_f", [NEXP, H, 512])
    edT_d = din("edT_f", [NEXP, 512, H])
    sgT_d = din("sgT_f", [H, H])
    suT_d = din("suT_f", [H, H])
    sdT_d = din("sdT_f", [H, H])
    identb_d = din("ident_bf", [128, 128], BF16)
    identf_d = din("ident_f", [128, 128], F32)

    out_d = nc.dram_tensor("out", [TOK, H], F32, kind="ExternalOutput").ap()

    def scr(name, shape, dt):
        return nc.dram_tensor(name, shape, dt, kind="Internal").ap()

    winbf_d = scr("winbf", [H, 2 * DIN], BF16)
    acc_d = scr("acc_scr", [TOK, H], F32)
    woutbf_d = scr("woutbf", [DIN, H], BF16)
    bc_d = scr("bc_scr", [NCHUNK, 32, CH], BF16)
    ub_d = scr("ub_scr", [NCHUNK, 128, NDT * CH], BF16)
    sres_d = scr("sres_scr", [NCHUNK, 128, NDT * CH], BF16)
    y_d = scr("y_scr", [NCHUNK, 128, NDT * CH], BF16)
    cums_d = scr("cums_scr", [NCHUNK, 128, NDT * CH], BF16)
    x1_d = scr("x1_scr", [TOK, H], F32)
    wt_d = scr("wt_scr", [NEXP, TOK], BF16)
    yg_d = scr("yg_scr", [NCHUNK, 128, NDT * CH], BF16)
    cc_ins = [scr(f"cc_in{s}", [128, 272], F32) for s in range(2)]
    cc_outs = [scr(f"cc_out{s}", [8 * 128, 272], F32) for s in range(2)]

    P = Prog()
    ES = ExitStack()

    def sbG(name, shape, dt):
        return ES.enter_context(nc.sbuf_tensor(name, shape, dt))

    def psG(name, shape, dt):
        return ES.enter_context(nc.psum_tensor(name, shape, dt))

    binl = sbG("binl", [128, 32], F32)
    wconv = sbG("wconv", [128, NDT * TW], F32)
    bconv = sbG("bconv", [128, NDT], F32)
    wxbf = sbG("wxbf", [128, NDT * (DTR + 2 * NST)], BF16)
    wdtbf = sbG("wdtbf", [64, DIN], BF16)
    negA = sbG("negA", [128, NDT * NST], F32)
    dl = sbG("dl", [128, NDT], F32)
    gwTbf = sbG("gwTbf", [128, NKH * NEXP], BF16)
    gbb = sbG("gbb", [128, NEXP], F32)
    identb = sbG("identb", [128, 128], BF16)
    hcarry = sbG("hcarry", [128, NDT * NST], F32)
    ccarry = sbG("ccarry", [128, NDT], F32)
    h_in = sbG("h_in", [128, NDT * NST], F32)
    maskb = sbG("maskb", [128, 1], F32)
    pmb = sbG("pmb", [128, 8], F32)
    consts = sbG("consts", [128, 8], F32)

    stat = sbG("stat", [128, 64], F32)
    tsA = [sbG(f"tsA{i}", [128, CH], F32) for i in range(2)]
    esA = [sbG("esA0", [128, CH], F32)]
    tsB = [sbG(f"tsB{i}", [128, CH], BF16) for i in range(2)]
    cbuf = [sbG(f"cbuf{i}", [128, CH], BF16) for i in range(2)]

    pb = [psG(f"pb{i}", [128, 512], F32) for i in range(7)]
    pbf = psG("pbf", [128, 1024], BF16)

    def V(fn, rd=(), wr=(), tag=""):
        P.emit("dve", fn, rd, wr, tag=tag)

    def A_(fn, rd=(), wr=(), tag=""):
        P.emit("act", fn, rd, wr, tag=tag)

    def PE(fn, rd=(), wr=(), tag=""):
        P.emit("pe", fn, rd, wr, tag=tag)

    def SY(fn, rd=(), wr=(), tag=""):
        P.emit("sync", fn, rd, wr, dma=True, tag=tag)

    def GD(fn, rd=(), wr=(), tag=""):
        P.emit("gdma", fn, rd, wr, dma=True, tag=tag)

    def PL(fn, rd=(), wr=(), inc=None, tag=""):
        P.emit("pool", fn, rd, wr, inc=inc, tag=tag)

    HC = [f"hcar{c}" for c in range(NDT * NST)]
    CC = [f"ccar{m}" for m in range(NDT)]

    V(lambda e: e.memset(consts[:, 0:1], 1e-5), wr=["consts"])
    V(lambda e: e.memset(consts[:, 1:2], -0.5), wr=["consts"])
    V(lambda e: e.memset(consts[:, 2:3], 1.0), wr=["consts"])
    V(lambda e: e.memset(hcarry[:], 0.0), wr=["hcarry"] + HC)
    V(lambda e: e.memset(ccarry[:], 0.0), wr=["ccarry"] + CC)
    V(lambda e: e.memset(h_in[:], 0.0), wr=["h_in"])

    SY(lambda e: e.dma_start(out=binl[:], in_=binl_d[:]), wr=["binl"])
    SY(lambda e: e.dma_start(out=wconv[:], in_=wconv_d[:]), wr=["wconv"])
    SY(lambda e: e.dma_start(out=bconv[:], in_=bconv_d[:]), wr=["bconv"])
    SY(lambda e: e.dma_start(out=dl[:], in_=dl_d[:]), wr=["dl"])
    SY(lambda e: e.dma_start(out=gbb[:], in_=gb_d.partition_broadcast(128)), wr=["gbb"])
    SY(lambda e: e.dma_start(out=identb[:], in_=identb_d[:]), wr=["identb"])
    GD(lambda e: e.dma_start(out=maskb[:], in_=mask_d.partition_broadcast(128)), wr=["maskb"])
    GD(lambda e: e.dma_start(out=pmb[:], in_=posm_d.rearrange("k o -> o k").partition_broadcast(128)), wr=["pmb"])
    GD(lambda e: e.dma_start(out=wxbf[:].rearrange("p (k m) -> p k m", m=DTR + 2 * NST), in_=wx_d.rearrange("(k p) m -> p k m", p=128)), wr=["wxbf"])
    GD(lambda e: e.dma_start(out=wdtbf[:], in_=wdt_d[:]), wr=["wdtbf"])
    GD(lambda e: e.dma_start(out=gwTbf[:].rearrange("p (k m) -> p k m", m=NEXP), in_=gwT_d.rearrange("(k p) m -> p k m", p=128)), wr=["gwTbf"])
    GD(lambda e: e.dma_start(out=winbf_d[:], in_=win_d[:]), wr=["winbf"])
    GD(lambda e: e.dma_start(out=woutbf_d[:], in_=wout_d[:]), wr=["woutbf"])

    lnh = sbG("lnh", [128, NDT * NST], F32)
    sgnh = sbG("sgnh", [128, NDT * NST], F32)
    e2bf = [sbG(f"e2bf{i}", [128, CH], BF16) for i in range(2)]
    agr = [sbG("agr0", [128, 272], F32)]
    V(lambda e: e.memset(consts[:, 4:5], 1e-38), rd=["consts"], wr=["consts"])

    HT = TOK // 2
    xn2T = sbG("xn2T", [128, NKH * HT], BF16)

    AB = ExitStack()

    def sbA(name, shape, dt):
        return AB.enter_context(nc.sbuf_tensor(name, shape, dt))

    alog_sb = sbA("alog_sb", [128, NDT * NST], F32)
    xc2 = [sbA(f"xc2_{i}", [128, 520], BF16) for i in range(2)]
    tailb = sbA("tailb", [128, NDT * 4], BF16)
    xf = [sbA(f"xf_{i}", [128, H], F32) for i in range(2)]
    xn = [sbA(f"xn_{i}", [128, H], BF16) for i in range(2)]
    xnT = sbA("xnT", [128, NKH * CH], BF16)
    wbufs = [sbA(f"wbuf{i}", [128, NKH * 128], BF16) for i in range(2)]
    sresT = [sbA(f"sresT{i}", [128, CH], BF16) for i in range(2)]
    ubfT = sbA("ubfT", [128, NDT * CH], BF16)
    dd = sbA("dd", [128, 2 * NDT * CH], BF16)
    deltaT = dd[:, 0:NDT * CH]
    dubT = dd[:, NDT * CH:2 * NDT * CH]
    woutali = dd
    dtbfT = sbA("dtbfT", [64, CH], BF16)
    bcs = sbA("bcs", [32, CH], BF16)
    y_sb = ubfT
    cums_sb = sbA("cums_sb", [128, CH], BF16)
    a2 = [sbA(f"a2_{i}", [128, CH], F32) for i in range(2)]
    bu2 = [sbA(f"bu2_{i}", [128, CH], BF16) for i in range(2)]
    hs2 = [sbA(f"hs2_{i}", [128, CH], BF16) for i in range(2)]
    bbuf = [sbA(f"bbuf{i}", [128, CH], BF16) for i in range(2)]
    xh_s, xhn = xf[0], xn[0]
    xhT = sbA("xhT", [128, NKH * 3], BF16)

    SY(lambda e: e.dma_start(out=alog_sb[:], in_=alog_d[:]), wr=["alog"])
    A_(lambda e: e.activation(out=negA[:], in_=alog_sb[:], func=AF.Exp), rd=["alog"], wr=["negA"])
    V(lambda e: e.tensor_scalar(out=negA[:], in0=negA[:], scalar1=-1.0, scalar2=None, op0=ALU.mult),
      rd=["negA"], wr=["negA"])

    def emit_ln(src_ap, dst_ap, nrow, src_key, dst_key, c0=0):
        A_(lambda e: e.activation(out=dst_ap, in_=src_ap, func=AF.Square,
                                  accum_out=stat[0:nrow, c0 + 0:c0 + 1]), rd=[src_key, dst_key], wr=[dst_key, f"st0_{c0}"])
        V(lambda e: e.tensor_reduce(out=stat[0:nrow, c0 + 1:c0 + 2], in_=src_ap, axis=AX, op=ALU.add),
          rd=[src_key], wr=[f"st1_{c0}"])
        V(lambda e: e.tensor_scalar(out=stat[0:nrow, c0 + 2:c0 + 3], in0=stat[0:nrow, c0 + 1:c0 + 2], scalar1=1.0 / H,
                                    scalar2=None, op0=ALU.mult), rd=[f"st1_{c0}"], wr=[f"st2_{c0}"])
        V(lambda e: e.tensor_tensor(out=stat[0:nrow, c0 + 3:c0 + 4], in0=stat[0:nrow, c0 + 2:c0 + 3],
                                    in1=stat[0:nrow, c0 + 2:c0 + 3], op=ALU.mult), rd=[f"st2_{c0}"], wr=[f"st3_{c0}"])
        V(lambda e: e.scalar_tensor_tensor(out=stat[0:nrow, c0 + 4:c0 + 5], in0=stat[0:nrow, c0 + 0:c0 + 1],
                                           scalar=1.0 / H, in1=stat[0:nrow, c0 + 3:c0 + 4],
                                           op0=ALU.mult, op1=ALU.subtract),
          rd=[f"st0_{c0}", f"st3_{c0}"], wr=[f"st4_{c0}"])
        A_(lambda e: e.activation(out=stat[0:nrow, c0 + 5:c0 + 6], in_=stat[0:nrow, c0 + 4:c0 + 5], func=AF.Ln,
                                  bias=consts[0:nrow, 0:1]), rd=[f"st4_{c0}", "consts"], wr=[f"st5_{c0}"])
        A_(lambda e: e.activation(out=stat[0:nrow, c0 + 6:c0 + 7], in_=stat[0:nrow, c0 + 5:c0 + 6], func=AF.Exp,
                                  scale=consts[0:nrow, 1:2]), rd=[f"st5_{c0}", "consts"], wr=[f"st6_{c0}"])
        V(lambda e: e.tensor_tensor(out=stat[0:nrow, c0 + 7:c0 + 8], in0=stat[0:nrow, c0 + 2:c0 + 3],
                                    in1=stat[0:nrow, c0 + 6:c0 + 7], op=ALU.mult), rd=[f"st2_{c0}", f"st6_{c0}"], wr=[f"st7_{c0}"])
        V(lambda e: e.tensor_scalar(out=dst_ap, in0=src_ap, scalar1=stat[0:nrow, c0 + 6:c0 + 7],
                                    scalar2=stat[0:nrow, c0 + 7:c0 + 8], op0=ALU.mult, op1=ALU.subtract),
          rd=[src_key, f"st6_{c0}", f"st7_{c0}"], wr=[dst_key])

    def xrow(k):
        return (k // 2) * 1027 + 3 + (k % 2) * CH

    def emit_hist(s):
        SY(lambda e: e.dma_start(out=xh_s[0:3, :], in_=x_d[s * 1027:s * 1027 + 3, :]),
           rd=["xf_0"], wr=["xf_0"])
        emit_ln(xh_s[0:3, :], xhn[0:3, :], 3, "xf_0", "xn_0", c0=32)
        for kk in range(NKH):
            PE(lambda e, kk=kk: e.transpose(pbf[:, kk * 128:kk * 128 + 3],
                                            xhn[0:3, kk * 128:(kk + 1) * 128], identb[0:3, 0:3]),
               rd=["xn_0", "identb"], wr=["PBF"])
            A_(lambda e, kk=kk: e.copy(out=xhT[:, kk * 3:(kk + 1) * 3], in_=pbf[:, kk * 128:kk * 128 + 3]),
               rd=["PBF", "xhT"], wr=["xhT"])
        for m in range(NDT):
            wb = wbufs[m % 2]
            SY(lambda e, m=m, wb=wb: e.dma_start(
                out=wb[:], in_=winbf_d.rearrange("(k p) f -> p k f", p=128)[:, :, m * 128:(m + 1) * 128]),
               rd=["winbf", f"wb{m % 2}"], wr=[f"wb{m % 2}"])
            for kk in range(NKH):
                PE(lambda e, m=m, kk=kk, wb=wb: e.matmul(
                    pb[5][0:128, 0:3], wb[:, kk * 128:(kk + 1) * 128], xhT[:, kk * 3:(kk + 1) * 3],
                    start=(kk == 0), stop=(kk == NKH - 1)),
                   rd=[f"wb{m % 2}", "xhT"], wr=["PB5"])
            A_(lambda e, m=m: e.activation(out=tsA[0][:, 0:3], in_=pb[5][0:128, 0:3], func=AF.Identity,
                                           bias=binl[:, m:m + 1]), rd=["PB5", "binl", "tsA0"], wr=["tsA0"])
            V(lambda e, m=m: e.tensor_scalar(out=tailb[:, m * 4:m * 4 + 3], in0=tsA[0][:, 0:3],
                                             scalar1=maskb[:], scalar2=None, op0=ALU.mult),
              rd=["tsA0", "maskb", f"xch{m}"], wr=[f"xch{m}"])

    def phase_A(k):
        for p in range(4):
            xfp, xfk = xf[p % 2], f"xf_{p % 2}"
            xnp, xnk = xn[p % 2], f"xn_{p % 2}"
            SY(lambda e, p=p, xfp=xfp: e.dma_start(out=xfp[:],
                                                   in_=x_d[xrow(k) + p * 128:xrow(k) + (p + 1) * 128, :]),
               rd=[xfk], wr=[xfk], tag=f"ldx{k}")
            emit_ln(xfp[:], xnp[:], 128, xfk, xnk, c0=(p % 2) * 8)
            for kk in range(NKH):
                PE(lambda e, kk=kk, xnp=xnp: e.transpose(
                    pbf[:, (kk % 8) * 128:(kk % 8) * 128 + 128],
                    xnp[:, kk * 128:(kk + 1) * 128], identb[:]),
                   rd=[xnk, "identb"], wr=["PBF"])
                A_(lambda e, p=p, kk=kk: e.copy(
                    out=xnT[:, kk * CH + p * 128:kk * CH + (p + 1) * 128],
                    in_=pbf[:, (kk % 8) * 128:(kk % 8) * 128 + 128]),
                   rd=["PBF"], wr=[f"xnT{kk}"])
        for m in range(2 * NDT):
            wb = wbufs[m % 2]
            SY(lambda e, m=m, wb=wb: e.dma_start(
                out=wb[:], in_=winbf_d.rearrange("(k p) f -> p k f", p=128)[:, :, m * 128:(m + 1) * 128]),
               rd=["winbf"], wr=[f"wb{m % 2}"])
            bank, bkey = pb[5 + (m % 2)], f"PB{5 + (m % 2)}"
            for kk in range(NKH):
                PE(lambda e, m=m, kk=kk, wb=wb, bank=bank: e.matmul(
                    bank[:], wb[:, kk * 128:(kk + 1) * 128], xnT[:, kk * CH:(kk + 1) * CH],
                    start=(kk == 0), stop=(kk == NKH - 1)),
                   rd=[f"wb{m % 2}", f"xnT{kk}"], wr=[bkey])
            if m < NDT:
                xcp, xck = xc2[m % 2], f"xc2_{m % 2}"
                A_(lambda e, m=m, bank=bank, xcp=xcp: e.activation(
                    out=xcp[:, 3:3 + CH], in_=bank[:], func=AF.Identity,
                    bias=binl[:, m:m + 1]), rd=[bkey, "binl", xck], wr=[xck])
                V(lambda e, m=m, xcp=xcp: e.tensor_copy(out=xcp[:, 0:3],
                                                        in_=tailb[:, m * 4:m * 4 + 3]),
                  rd=[f"xch{m}", xck], wr=[xck])
                cs, ck = tsA[m % 2], f"tsA{m % 2}"
                V(lambda e, m=m, cs=cs, xcp=xcp: e.tensor_scalar(
                    out=cs[:], in0=xcp[:, 0:CH], scalar1=wconv[:, m * TW:m * TW + 1],
                    scalar2=None, op0=ALU.mult), rd=[xck, "wconv", ck], wr=[ck])
                for kc in range(1, TW):
                    V(lambda e, m=m, kc=kc, cs=cs, xcp=xcp: e.scalar_tensor_tensor(
                        out=cs[:], in0=xcp[:, kc:kc + CH],
                        scalar=wconv[:, m * TW + kc:m * TW + kc + 1], in1=cs[:],
                        op0=ALU.mult, op1=ALU.add),
                      rd=[xck, "wconv", ck], wr=[ck])
                A_(lambda e, m=m, cs=cs: e.activation(out=ubfT[:, m * CH:(m + 1) * CH], in_=cs[:],
                                                      func=AF.Silu, bias=bconv[:, m:m + 1]),
                   rd=[ck, "bconv", f"ubf{m}", f"y{m}"], wr=[f"ubf{m}", f"y{m}"])
                V(lambda e, m=m, xcp=xcp: e.tensor_copy(out=tailb[:, m * 4:m * 4 + 3],
                                                        in_=xcp[:, CH:CH + 3]),
                  rd=[xck], wr=[f"xch{m}"])
            else:
                mm = m - NDT
                srp, srk = sresT[mm % 2], f"sres{mm % 2}"
                A_(lambda e, m=m, bank=bank, srp=srp: e.activation(
                    out=srp[:], in_=bank[:], func=AF.Silu,
                    bias=binl[:, m:m + 1]), rd=[bkey, "binl", srk], wr=[srk])
                SY(lambda e, mm=mm, srp=srp: e.dma_start(out=sres_d[k, :, mm * CH:(mm + 1) * CH],
                                                         in_=srp[:]),
                   rd=[srk], wr=[f"sresd{k}"])
        SY(lambda e: e.dma_start(out=ub_d[k], in_=ubfT[:]),
           rd=[f"ubf{m}" for m in range(NDT)], wr=[f"ubd{k}"])
        for kk in range(NDT):
            PE(lambda e, kk=kk: e.matmul(pb[6][0:96, 0:CH], wxbf[:, kk * 96:(kk + 1) * 96],
                                         ubfT[:, kk * CH:(kk + 1) * CH],
                                         start=(kk == 0), stop=(kk == NDT - 1)),
               rd=[f"ubf{kk}", "wxbf"], wr=["PB6"])
        A_(lambda e: e.copy(out=dtbfT[:], in_=pb[6][0:64, 0:CH]), rd=["PB6", "dtbf"], wr=["dtbf"])
        A_(lambda e: e.copy(out=bcs[:], in_=pb[6][64:96, 0:CH]), rd=["PB6", "bcs"], wr=["bcs"])
        GD(lambda e: e.dma_start(out=bc_d[k], in_=bcs[:]), rd=["bcs"], wr=[f"bcd{k}"])
        for m in range(NDT):
            bank, bkey = pb[5 + (m % 2)], f"PB{5 + (m % 2)}"
            PE(lambda e, m=m, bank=bank: e.matmul(
                bank[:], wdtbf[:, m * 128:(m + 1) * 128],
                dtbfT[:], start=True, stop=True),
               rd=["dtbf", "wdtbf"], wr=[bkey])
            ej = esA[0]
            ejk = "esA0"
            A_(lambda e, bank=bank, ej=ej: e.activation(out=ej[:], in_=bank[:], func=AF.Exp),
               rd=[bkey, ejk], wr=[ejk])
            A_(lambda e, m=m, ej=ej: e.activation(out=deltaT[:, m * CH:(m + 1) * CH], in_=ej[:],
                                           func=AF.Ln, bias=consts[:, 2:3]),
               rd=[ejk, "consts", "woutsb"], wr=[f"del{m}"])
            V(lambda e, m=m: e.tensor_tensor(out=dubT[:, m * CH:(m + 1) * CH],
                                             in0=deltaT[:, m * CH:(m + 1) * CH],
                                             in1=ubfT[:, m * CH:(m + 1) * CH], op=ALU.mult),
              rd=[f"del{m}", f"ubf{m}", "woutsb"], wr=[f"dub{m}"])

    def phase_B(k, n_range, m_range=None):
        if m_range is None:
            m_range = range(NDT)
        do_cums = n_range.start == 0 and m_range.start == 0
        for m in (range(NDT) if do_cums else ()):
            V(lambda e, m=m: e.tensor_tensor_scan(out=cums_sb[:], data0=consts[:, 2:3].broadcast_to((128, CH)),
                                                  data1=deltaT[:, m * CH:(m + 1) * CH],
                                                  initial=ccarry[:, m:m + 1],
                                                  op0=ALU.mult, op1=ALU.add),
              rd=[f"del{m}", "consts", f"ccar{m}"], wr=["cums"])
            V(lambda e, m=m: e.tensor_copy(out=ccarry[:, m:m + 1], in_=cums_sb[:, CH - 1:CH]),
              rd=["cums"], wr=[f"ccar{m}"])
            SY(lambda e, m=m: e.dma_start(out=cums_d[k, :, m * CH:(m + 1) * CH], in_=cums_sb[:]),
               rd=["cums"], wr=[f"cumsd{k}"])
        def load_bb(n):
            bB, bC = bbuf[n % 2], cbuf[n % 2]
            SY(lambda e, n=n, bB=bB: e.dma_start(
                out=bB[:], in_=bc_d[k, n:n + 1, :].partition_broadcast(128)),
               rd=[f"bcd{k}", f"bb{n % 2}"], wr=[f"bb{n % 2}"])
            SY(lambda e, n=n, bC=bC: e.dma_start(
                out=bC[:], in_=bc_d[k, NST + n:NST + n + 1, :].partition_broadcast(128)),
               rd=[f"bcd{k}", f"cb{n % 2}"], wr=[f"cb{n % 2}"])
        if do_cums:
            load_bb(0)
        for n in n_range:
            bB, bC = bbuf[n % 2], cbuf[n % 2]
            if n + 1 < NST and m_range.start == 0:
                load_bb(n + 1)
            for m in m_range:
                col = m * NST + n
                j = (n * NDT + m) % 2
                aj, bj, hj, tj = a2[j], bu2[j], hs2[j], tsB[j]
                ak, bk2, hk, tk = f"a2_{j}", f"bu2_{j}", f"hs2_{j}", f"tsB{j}"
                A_(lambda e, m=m, col=col, aj=aj: e.activation(
                    out=aj[:], in_=deltaT[:, m * CH:(m + 1) * CH], func=AF.Exp,
                    scale=negA[:, col:col + 1]), rd=[f"del{m}", "negA", ak], wr=[ak])
                V(lambda e, m=m, bB=bB, bj=bj: e.tensor_tensor(
                    out=bj[:], in0=dubT[:, m * CH:(m + 1) * CH], in1=bB[:], op=ALU.mult),
                  rd=[f"dub{m}", f"bb{n % 2}", bk2], wr=[bk2])
                V(lambda e, col=col, aj=aj, bj=bj, hj=hj: e.tensor_tensor_scan(
                    out=hj[:], data0=aj[:], data1=bj[:],
                    initial=hcarry[:, col:col + 1], op0=ALU.mult, op1=ALU.add),
                  rd=[ak, bk2, f"hcar{col}", hk], wr=[hk])
                V(lambda e, col=col, hj=hj: e.tensor_copy(out=hcarry[:, col:col + 1],
                                                          in_=hj[:, CH - 1:CH]),
                  rd=[hk], wr=[f"hcar{col}"])
                V(lambda e, bC=bC, hj=hj, tj=tj: e.tensor_tensor(out=tj[:], in0=hj[:], in1=bC[:],
                                                                 op=ALU.mult),
                  rd=[hk, f"cb{n % 2}", tk], wr=[tk])
                if n == 0:
                    V(lambda e, m=m, tj=tj: e.tensor_copy(out=y_sb[:, m * CH:(m + 1) * CH], in_=tj[:]),
                      rd=[tk, f"y{m}", f"ubf{m}"], wr=[f"y{m}", f"ubf{m}"])
                else:
                    V(lambda e, m=m, tj=tj: e.tensor_tensor(out=y_sb[:, m * CH:(m + 1) * CH],
                                                            in0=y_sb[:, m * CH:(m + 1) * CH],
                                                            in1=tj[:], op=ALU.add),
                      rd=[tk, f"y{m}", f"ubf{m}"], wr=[f"y{m}", f"ubf{m}"])
        if n_range.stop == NST and m_range.stop == NDT:
            SY(lambda e: e.dma_start(out=y_d[k], in_=y_sb[:]),
               rd=[f"y{m}" for m in range(NDT)], wr=[f"yd{k}"])

    def emit_CC(s):
        Hc = a2[0][:, 0:256]
        Pt = a2[1][:, 0:256]
        ci, co = cc_ins[s], cc_outs[s]
        SY(lambda e: e.dma_start(out=ci[:, 0:256], in_=hcarry[:]), rd=HC + ["ccin"], wr=["ccin"])
        SY(lambda e: e.dma_start(out=ci[:, 256:272], in_=ccarry[:]), rd=CC + ["ccin"], wr=["ccin"])
        if do_cc:
            PL(lambda e: e.collective_compute(
                "AllGather", ALU.bypass, replica_groups=[[0, 1, 2, 3, 4, 5, 6, 7]],
                ins=[ci[:]], outs=[co[:]]), rd=["ccin"], wr=["ccout"], inc=1)
        else:
            for r in range(8):
                GD(lambda e, r=r: e.dma_start(out=co[r * 128:(r + 1) * 128, :], in_=ci[:]),
                   rd=["ccin"], wr=["ccout"])
        def load_agr(r):
            ag, agk = agr[0], "agr0"
            SY(lambda e, r=r, ag=ag: e.dma_start(out=ag[:], in_=co[r * 128:(r + 1) * 128, :]),
               rd=["ccout", agk], wr=[agk])
        load_agr(0)
        ag0 = agr[0]
        V(lambda e, ag0=ag0: e.tensor_copy(out=Hc, in_=ag0[:, 0:256]), rd=["agr0", "Hc", "a2_0"], wr=["Hc", "a2_0"])
        V(lambda e: e.tensor_scalar(out=h_in[:], in0=Hc, scalar1=pmb[:, 1:2], scalar2=None,
                                    op0=ALU.mult), rd=["Hc", "pmb", "h_in"], wr=["h_in"])
        for r in range(1, 7):
            ag, agk = agr[0], "agr0"
            load_agr(r)
            V(lambda e, ag=ag: e.tensor_tensor(
                out=Pt.rearrange("p (d n) -> p d n", n=NST),
                in0=negA[:].rearrange("p (d n) -> p d n", n=NST),
                in1=ag[:, 256:272].unsqueeze(2).broadcast_to((128, NDT, NST)),
                op=ALU.mult), rd=[agk, "negA", "Pt", "a2_1"], wr=["Pt", "a2_1"])
            A_(lambda e: e.activation(out=Pt, in_=Pt, func=AF.Exp), rd=["Pt", "a2_1"], wr=["Pt", "a2_1"])
            V(lambda e: e.tensor_tensor(out=Hc, in0=Pt, in1=Hc, op=ALU.mult),
              rd=["Pt", "Hc", "a2_0"], wr=["Hc", "a2_0"])
            V(lambda e, ag=ag: e.tensor_tensor(out=Hc, in0=Hc, in1=ag[:, 0:256],
                                               op=ALU.add), rd=["Hc", agk, "a2_0"], wr=["Hc", "a2_0"])
            V(lambda e, r=r: e.scalar_tensor_tensor(out=h_in[:], in0=Hc, scalar=pmb[:, r + 1:r + 2],
                                                    in1=h_in[:], op0=ALU.mult, op1=ALU.add),
              rd=["Hc", "pmb", "h_in"], wr=["h_in"])
        A_(lambda e: e.activation(out=lnh[:], in_=h_in[:], func=AF.Abs), rd=["h_in", "lnh"], wr=["lnh"])
        A_(lambda e: e.activation(out=sgnh[:], in_=h_in[:], func=AF.Sign), rd=["h_in", "sgnh"], wr=["sgnh"])
        A_(lambda e: e.activation(out=lnh[:], in_=lnh[:], func=AF.Ln, bias=consts[:, 4:5]),
           rd=["lnh", "consts"], wr=["lnh"])


    def make_C(sfx):
        Creg = ExitStack()

        def sbC(name, shape, dt):
            return Creg.enter_context(nc.sbuf_tensor(name + sfx, shape, dt))

        t = {}
        t["woutbf2"] = woutali
        t["Cball"] = sbC("Cball", [128, NST * CH], BF16)
        t["cum2"] = [sbC(f"cum2_{i}", [128, CH], BF16) for i in range(2)]
        t["ub2"] = [sbC(f"ub2_{i}", [128, CH], BF16) for i in range(2)]
        t["sr2"] = [sbC(f"sr2_{i}", [128, CH], BF16) for i in range(2)]
        t["yCm"] = [sbC(f"yCm_{i}", [128, CH], F32) for i in range(2)]
        t["ygf"] = sbC("ygf", [128, NDT * CH], BF16)
        t["xfl"] = [sbC(f"xfl_{i}", [128, H], F32) for i in range(2)]
        t["xn2r"] = [sbC(f"xn2r_{i}", [128, H], BF16) for i in range(2)]
        GD(lambda e: e.dma_start(out=t["woutbf2"][:].rearrange("p (k f) -> p k f", f=H),
                                 in_=woutbf_d.rearrange("(k p) f -> p k f", p=128)),
           rd=["woutbf", "woutsb"],
           wr=["woutsb"] + [f"del{m}" for m in range(NDT)] + [f"dub{m}" for m in range(NDT)])
        return Creg, t

    def phase_C(k, t):
        SY(lambda e: e.dma_start(
            out=t["Cball"][:].rearrange("p (n l) -> p n l", l=CH),
            in_=bc_d[k, NST:2 * NST, :].unsqueeze(0).partition_broadcast(128)),
           rd=[f"bcd{k}", "Cball"], wr=["Cball"])
        for m in range(NDT):
            jm = m % 2
            cum, cumk = t["cum2"][jm], f"cum2_{jm}"
            ub, ubk = t["ub2"][jm], f"ub2_{jm}"
            sr, srk = t["sr2"][jm], f"sr2_{jm}"
            yc, yck = t["yCm"][jm], f"yCm_{jm}"
            SY(lambda e, m=m, cum=cum: e.dma_start(out=cum[:], in_=cums_d[k, :, m * CH:(m + 1) * CH]),
               rd=[f"cumsd{k}", cumk], wr=[cumk])
            SY(lambda e, m=m, ub=ub: e.dma_start(out=ub[:], in_=ub_d[k, :, m * CH:(m + 1) * CH]),
               rd=[f"ubd{k}", ubk], wr=[ubk])
            SY(lambda e, m=m, sr=sr: e.dma_start(out=sr[:], in_=sres_d[k, :, m * CH:(m + 1) * CH]),
               rd=[f"sresd{k}", srk], wr=[srk])
            GD(lambda e, m=m, yc=yc: e.dma_start(out=yc[:], in_=y_d[k, :, m * CH:(m + 1) * CH]),
               rd=[f"yd{k}", yck], wr=[yck])
            for n in range(NST):
                col = m * NST + n
                j = (m * NST + n) % 2
                ej, tj = e2bf[j], tsB[j]
                ek, tk = f"e2bf{j}", f"tsB{j}"
                A_(lambda e, col=col, ej=ej, cum=cum: e.activation(
                    out=ej[:], in_=cum[:], func=AF.Exp,
                    scale=negA[:, col:col + 1], bias=lnh[:, col:col + 1]),
                   rd=[cumk, "negA", "lnh", ek], wr=[ek])
                V(lambda e, n=n, ej=ej, tj=tj: e.tensor_tensor(
                    out=tj[:], in0=ej[:], in1=t["Cball"][:, n * CH:(n + 1) * CH], op=ALU.mult),
                  rd=[ek, "Cball", tk], wr=[tk])
                V(lambda e, col=col, tj=tj, yc=yc: e.scalar_tensor_tensor(
                    out=yc[:], in0=tj[:], scalar=sgnh[:, col:col + 1],
                    in1=yc[:], op0=ALU.mult, op1=ALU.add),
                  rd=[tk, "sgnh", yck], wr=[yck])
            V(lambda e, m=m, ub=ub, yc=yc: e.scalar_tensor_tensor(
                out=yc[:], in0=ub[:], scalar=dl[:, m:m + 1], in1=yc[:],
                op0=ALU.mult, op1=ALU.add), rd=[ubk, "dl", yck], wr=[yck])
            V(lambda e, m=m, sr=sr, yc=yc: e.tensor_tensor(
                out=t["ygf"][:, m * CH:(m + 1) * CH], in0=yc[:], in1=sr[:], op=ALU.mult),
              rd=[yck, srk, f"yg{m}"], wr=[f"yg{m}"])
        for ls in range(4):
            jl = ls % 2
            xfl, xfk = t["xfl"][jl], f"xfl_{jl}"
            xnr, xnk = t["xn2r"][jl], f"xn2r_{jl}"
            SY(lambda e, ls=ls, xfl=xfl: e.dma_start(
                out=xfl[:], in_=x_d[xrow(k) + ls * 128:xrow(k) + (ls + 1) * 128, :]),
               rd=[xfk], wr=[xfk])
            for hh in range(2):
                bank, bkey = pb[(ls * 2 + hh) % 4], f"PB{(ls * 2 + hh) % 4}"
                for m in range(NDT):
                    PE(lambda e, ls=ls, hh=hh, m=m, bank=bank: e.matmul(
                        bank[:], t["ygf"][:, m * CH + ls * 128:m * CH + (ls + 1) * 128],
                        t["woutbf2"][:, m * H + hh * 512:m * H + (hh + 1) * 512],
                        start=(m == 0), stop=(m == NDT - 1)),
                       rd=[f"yg{m}", "woutsb"], wr=[bkey])
                V(lambda e, ls=ls, hh=hh, bank=bank, xfl=xfl: e.tensor_tensor(
                    out=xfl[:, hh * 512:(hh + 1) * 512],
                    in0=xfl[:, hh * 512:(hh + 1) * 512],
                    in1=bank[:], op=ALU.add), rd=[bkey, xfk], wr=[xfk])
            SY(lambda e, ls=ls, xfl=xfl: e.dma_start(
                out=x1_d[k * CH + ls * 128:k * CH + (ls + 1) * 128, :], in_=xfl[:]),
               rd=[xfk], wr=[f"x1d{k}"])
            emit_ln(xfl[:], xnr[:], 128, xfk, xnk, c0=jl * 8)
            for kk in range(NKH):
                PE(lambda e, ls=ls, kk=kk, xnr=xnr: e.transpose(
                    pbf[:, ls * 128:(ls + 1) * 128],
                    xnr[:, kk * 128:(kk + 1) * 128], identb[:]),
                   rd=[xnk, "identb"], wr=["PBF"])
                A_(lambda e, ls=ls, kk=kk: e.copy(
                    out=xn2T[:, kk * HT + (k % 2) * CH + ls * 128:kk * HT + (k % 2) * CH + (ls + 1) * 128],
                    in_=pbf[:, ls * 128:(ls + 1) * 128]),
                   rd=["PBF"], wr=[f"x2T{kk}"])

    NQ = 8

    def emit_D(s, sfx):
        Dreg = ExitStack()

        def sbD(name, shape, dt):
            return Dreg.enter_context(nc.sbuf_tensor(name + sfx, shape, dt))

        scores = sbD("scores", [128, NQ * NEXP], F32)
        choice = sbD("choice", [128, NQ * NEXP], F32)
        msk = sbD("msk", [128, NQ * NEXP], F32)
        Wt = sbD("Wt", [128, NQ * NEXP], F32)
        eqs = sbD("eqs", [128, NQ * NEXP], F32)
        g1 = sbD("g1", [128, NQ * NGRP], F32)
        g2 = sbD("g2", [128, NQ * NGRP], F32)
        gs = sbD("gs", [128, NQ * NGRP], F32)
        thr = sbD("thr", [128, NQ], F32)
        Wtb = sbD("Wtb", [128, NQ * NEXP], BF16)
        WT = sbD("WT", [32, NQ * 128], BF16)

        def r3(ap):
            return ap.rearrange("p (q e) -> p q e", e=NEXP)

        def r3g(ap):
            return ap.rearrange("p (q g) -> p q g", g=NGRP)

        def r4(ap):
            return ap.rearrange("p (q e) -> p q e", e=NEXP).rearrange(
                "p q (g j) -> p q g j", j=NEXP // NGRP)

        def bq(ap, last):
            return ap[:, 0:NQ].unsqueeze(2).broadcast_to((128, NQ, last))

        def bqg(ap):
            return ap.rearrange("p (q g) -> p q g", g=NGRP).unsqueeze(3).broadcast_to(
                (128, NQ, NGRP, NEXP // NGRP))

        for ql in range(NQ):
            q = s * NQ + ql
            bank, bkey = pb[ql % 2], f"PB{ql % 2}"
            for kk in range(NKH):
                PE(lambda e, ql=ql, kk=kk, bank=bank: e.matmul(
                    bank[:, 0:NEXP], xn2T[:, kk * HT + ql * 128:kk * HT + (ql + 1) * 128],
                    gwTbf[:, kk * NEXP:(kk + 1) * NEXP], start=(kk == 0), stop=(kk == NKH - 1)),
                   rd=[f"x2T{kk}", "gwTbf"], wr=[bkey])
            A_(lambda e, ql=ql, bank=bank: e.activation(
                out=scores[:, ql * NEXP:(ql + 1) * NEXP], in_=bank[:, 0:NEXP], func=AF.Sigmoid),
               rd=[bkey, "scores"], wr=["scores"])
        V(lambda e: e.tensor_tensor(out=r3(choice), in0=r3(scores),
                                    in1=gbb[:].unsqueeze(1).broadcast_to((128, NQ, NEXP)),
                                    op=ALU.add), rd=["scores", "gbb", "choice"], wr=["choice"])
        V(lambda e: e.tensor_reduce(out=r3g(g1), in_=r4(choice), axis=AX, op=ALU.max),
          rd=["choice", "g1"], wr=["g1"])
        V(lambda e: e.tensor_tensor(out=r4(eqs), in0=r4(choice), in1=bqg(g1), op=ALU.is_ge),
          rd=["choice", "g1", "eqs"], wr=["eqs"])
        V(lambda e: e.scalar_tensor_tensor(out=r4(msk), in0=r4(eqs), scalar=-1e30, in1=r4(choice),
                                           op0=ALU.mult, op1=ALU.add), rd=["eqs", "choice", "msk"], wr=["msk"])
        V(lambda e: e.tensor_reduce(out=r3g(g2), in_=r4(msk), axis=AX, op=ALU.max),
          rd=["msk", "g2"], wr=["g2"])
        V(lambda e: e.tensor_tensor(out=r3g(gs), in0=r3g(g1), in1=r3g(g2), op=ALU.add),
          rd=["g1", "g2", "gs"], wr=["gs"])
        V(lambda e: e.tensor_reduce(out=thr[:, 0:NQ], in_=r3g(gs), axis=AX, op=ALU.max),
          rd=["gs", "thr"], wr=["thr"])
        V(lambda e: e.tensor_tensor(out=r3g(g1), in0=r3g(gs), in1=bq(thr, NGRP), op=ALU.is_ge),
          rd=["gs", "thr", "g1"], wr=["g1"])
        V(lambda e: e.scalar_tensor_tensor(out=r3g(g2), in0=r3g(g1), scalar=-1e30, in1=r3g(gs),
                                           op0=ALU.mult, op1=ALU.add), rd=["g1", "gs", "g2"], wr=["g2"])
        V(lambda e: e.tensor_reduce(out=thr[:, 0:NQ], in_=r3g(g2), axis=AX, op=ALU.max),
          rd=["g2", "thr"], wr=["thr"])
        V(lambda e: e.tensor_tensor(out=r3g(g1), in0=r3g(gs), in1=bq(thr, NGRP), op=ALU.is_ge),
          rd=["gs", "thr", "g1"], wr=["g1"])
        V(lambda e: e.tensor_tensor(out=r4(msk), in0=r4(choice), in1=bqg(g1), op=ALU.mult),
          rd=["choice", "g1", "msk"], wr=["msk"])
        for it in range(TOPK):
            V(lambda e: e.tensor_reduce(out=thr[:, 0:NQ], in_=r3(msk), axis=AX, op=ALU.max),
              rd=["msk", "thr"], wr=["thr"])
            if it < TOPK - 1:
                V(lambda e: e.tensor_tensor(out=r3(eqs), in0=r3(msk), in1=bq(thr, NEXP), op=ALU.is_ge),
                  rd=["msk", "thr", "eqs"], wr=["eqs"])
                V(lambda e: e.scalar_tensor_tensor(out=r3(msk), in0=r3(eqs), scalar=-1e30, in1=r3(msk),
                                                   op0=ALU.mult, op1=ALU.add), rd=["eqs", "msk"], wr=["msk"])
        V(lambda e: e.tensor_tensor(out=r4(eqs), in0=r4(choice), in1=bqg(g1), op=ALU.mult),
          rd=["choice", "g1", "eqs"], wr=["eqs"])
        V(lambda e: e.tensor_tensor(out=r3(eqs), in0=r3(eqs), in1=bq(thr, NEXP), op=ALU.is_ge),
          rd=["eqs", "thr"], wr=["eqs"])
        V(lambda e: e.tensor_tensor(out=r3(Wt), in0=r3(scores), in1=r3(eqs), op=ALU.mult),
          rd=["scores", "eqs", "Wt"], wr=["Wt"])
        V(lambda e: e.tensor_reduce(out=thr[:, 0:NQ], in_=r3(Wt), axis=AX, op=ALU.add),
          rd=["Wt", "thr"], wr=["thr"])
        V(lambda e: e.tensor_scalar(out=thr[:, 0:NQ], in0=thr[:, 0:NQ], scalar1=1e-20, scalar2=None,
                                    op0=ALU.add), rd=["thr"], wr=["thr"])
        V(lambda e: e.reciprocal(out=thr[:, 0:NQ], in_=thr[:, 0:NQ]), rd=["thr"], wr=["thr"])
        V(lambda e: e.scalar_tensor_tensor(out=r3(Wt), in0=r3(Wt), scalar=RSCALE, in1=bq(thr, NEXP),
                                           op0=ALU.mult, op1=ALU.mult), rd=["Wt", "thr"], wr=["Wt"])
        V(lambda e: e.tensor_copy(out=Wtb[:], in_=Wt[:]), rd=["Wt", "Wtb"], wr=["Wtb"])
        for ql in range(NQ):
            PE(lambda e, ql=ql: e.transpose(pbf[0:NEXP, 0:128], Wtb[:, ql * NEXP:(ql + 1) * NEXP],
                                            identb[:]), rd=["Wtb", "identb", "PBF"], wr=["PBF"])
            A_(lambda e, ql=ql: e.copy(out=WT[:, ql * 128:(ql + 1) * 128], in_=pbf[0:NEXP, 0:128]),
               rd=["PBF", "WT"], wr=["WT"])
        SY(lambda e: e.dma_start(out=wt_d[:, s * 1024:(s + 1) * 1024], in_=WT[:]),
           rd=["WT"], wr=[f"wtd{s}"])
        Dreg.close()

    def make_E(sfx):
        Ereg = ExitStack()

        def sbE(name, shape, dt):
            return Ereg.enter_context(nc.sbuf_tensor(name + sfx, shape, dt))

        t = {}
        t["mb"] = [[sbE(f"mb{i}{j}", [128, 4096], BF16) for j in range(2)] for i in range(2)]
        t["dscr"] = [sbE(f"dscr{i}", [128, CH], F32) for i in range(2)]
        t["tts"] = [sbE(f"tt{i}", [128, 8 * CH], BF16) for i in range(2)]
        t["Wb"] = [sbE(f"Wb{i}", [128, HT], BF16) for i in range(2)]
        t["sg"] = [sbE(f"sg_e{i}", [128, CH], BF16) for i in range(2)]
        t["ue"] = [sbE(f"ue_e{i}", [128, CH], BF16) for i in range(2)]
        t["ts"] = [sbE(f"ts_e{i}", [128, CH], BF16) for i in range(2)]
        t["xfF"] = [sbE(f"xfF{i}", [128, H], F32) for i in range(2)]
        return Ereg, t

    def load_mat(t, buf, key, eidx, which):
        if eidx < NEXP:
            dram = {"eg": egT_d, "eu": euT_d, "ed": edT_d}[which]
            wid = H if which == "ed" else 512
            GD(lambda e, dram=dram, wid=wid, buf=buf: e.dma_start(
                out=buf[:].rearrange("p (k i) -> p k i", i=wid),
                in_=dram[eidx].rearrange("(k p) i -> p k i", p=128)),
               rd=[key], wr=[key])
        else:
            hb = (eidx - NEXP) * 512
            if which == "eg":
                GD(lambda e, buf=buf: e.dma_start(out=buf[:].rearrange("p (k i) -> p k i", i=512),
                                                  in_=sgT_d[:, hb:hb + 512].rearrange("(k p) i -> p k i", p=128)),
                   rd=[key], wr=[key])
            elif which == "eu":
                GD(lambda e, buf=buf: e.dma_start(out=buf[:].rearrange("p (k i) -> p k i", i=512),
                                                  in_=suT_d[:, hb:hb + 512].rearrange("(k p) i -> p k i", p=128)),
                   rd=[key], wr=[key])
            else:
                GD(lambda e, buf=buf: e.dma_start(out=buf[:].rearrange("p (k i) -> p k i", i=H),
                                                  in_=sdT_d[hb:hb + 512, :].rearrange("(k p) i -> p k i", p=128)),
                   rd=[key], wr=[key])

    def load_Wb(t, s, si, eidx):
        if eidx < NEXP:
            SY(lambda e: e.dma_start(out=t["Wb"][si][:],
                                     in_=wt_d[eidx:eidx + 1, s * 1024:(s + 1) * 1024].partition_broadcast(128)),
               rd=[f"wtd{s}", f"Wbb{si}"], wr=[f"Wbb{si}"])

    def expert_gu(t, si, lc, routed, egb, egk, eub, euk):
        lo = (lc % 2) * CH
        for im in range(4):
            bank, bkey = pb[im % 2], f"PB{im % 2}"
            for kk in range(NKH):
                PE(lambda e, im=im, kk=kk, bank=bank, egb=egb: e.matmul(
                    bank[:], egb[:, kk * 512 + im * 128:kk * 512 + (im + 1) * 128],
                    xn2T[:, kk * HT + lo:kk * HT + lo + CH],
                    start=(kk == 0), stop=(kk == NKH - 1)),
                   rd=[egk, f"x2T{kk}"], wr=[bkey])
            bu_, buk = pb[2 + (im % 3)], f"PB{2 + (im % 3)}"
            for kk in range(NKH):
                PE(lambda e, im=im, kk=kk, bu_=bu_, eub=eub: e.matmul(
                    bu_[:], eub[:, kk * 512 + im * 128:kk * 512 + (im + 1) * 128],
                    xn2T[:, kk * HT + lo:kk * HT + lo + CH],
                    start=(kk == 0), stop=(kk == NKH - 1)),
                   rd=[euk, f"x2T{kk}"], wr=[buk])
            il = lc % 2
            tc = (il * 4 + im) * CH
            jj = im % 2
            sg, sgk = t["sg"][jj], f"sg_e{jj}"
            ue, uek = t["ue"][jj], f"ue_e{jj}"
            ts_, tsk = t["ts"][jj], f"ts_e{jj}"
            A_(lambda e, bank=bank, sg=sg: e.activation(out=sg[:], in_=bank[:], func=AF.Silu),
               rd=[bkey, sgk], wr=[sgk])
            A_(lambda e, bu_=bu_, ue=ue: e.copy(out=ue[:], in_=bu_[:]),
               rd=[buk, uek], wr=[uek])
            PL(lambda e, sg=sg, ue=ue, ts_=ts_: e.tensor_tensor(out=ts_[:], in0=sg[:], in1=ue[:],
                                                                op=ALU.mult),
               rd=[sgk, uek, tsk], wr=[tsk])
            if routed:
                PL(lambda e, tc=tc, lo=lo, si=si, ts_=ts_: e.tensor_tensor(
                    out=t["tts"][si][:, tc:tc + CH], in0=ts_[:],
                    in1=t["Wb"][si][:, lo:lo + CH], op=ALU.mult),
                   rd=[tsk, f"Wbb{si}", f"tt{si}_{il}_{im}"], wr=[f"tt{si}_{il}_{im}"])
            else:
                PL(lambda e, tc=tc, si=si, ts_=ts_: e.tensor_copy(out=t["tts"][si][:, tc:tc + CH],
                                                                  in_=ts_[:]),
                   rd=[tsk, f"tt{si}_{il}_{im}"], wr=[f"tt{si}_{il}_{im}"])

    def down_pair(t, lc, first_pair, ed0, ed0k, ed1, ed1k):
        eds = (ed0, ed1)
        edks = (ed0k, ed1k)
        for ls in range(4):
            q = lc * 4 + ls
            for hh in range(2):
                gi = (ls * 2 + hh) % 2
                bank, bkey = (pb[0], "PB0") if gi == 0 else (pb[1], "PB1")
                il = lc % 2
                mms = [(si, im) for si in (0, 1) for im in range(4)]
                for j, (si, im) in enumerate(mms):
                    tc = (il * 4 + im) * CH
                    PE(lambda e, si=si, im=im, ls=ls, hh=hh, bank=bank, tc=tc,
                       st=(j == 0), sp=(j == len(mms) - 1): e.matmul(
                        bank[:], t["tts"][si][:, tc + ls * 128:tc + (ls + 1) * 128],
                        eds[si][:, im * H + hh * 512:im * H + (hh + 1) * 512],
                        start=st, stop=sp),
                       rd=[f"tt{si}_{il}_{im}", edks[si]], wr=[bkey])
                ds = t["dscr"][(ls * 2 + hh) % 2]
                dk = f"dscr{(ls * 2 + hh) % 2}"
                A_(lambda e, bank=bank, ds=ds: e.copy(out=ds[:], in_=bank[:]),
                   rd=[bkey, dk], wr=[dk])
                aop = ALU.bypass if first_pair else ALU.add
                GD(lambda e, q=q, hh=hh, ds=ds, aop=aop: e.dma_start(
                    out=acc_d[q * 128:(q + 1) * 128, hh * 512:(hh + 1) * 512], in_=ds[:],
                    accum_op=aop),
                   rd=[dk, f"acc{q}_{hh}"], wr=[f"acc{q}_{hh}"])

    def emit_F(t, lc):
        ph = P.phase
        P.phase = "F"
        for q in range(lc * 4, lc * 4 + 4):
            j = q % 2
            xb, xk = t["xfF"][j], f"xfF{j}"
            SY(lambda e, q=q, xb=xb: e.dma_start(out=xb[:], in_=x1_d[q * 128:(q + 1) * 128, :]),
               rd=[f"x1d{q // 4}", xk], wr=[xk])
            GD(lambda e, q=q, xb=xb: e.dma_start(out=xb[:], in_=acc_d[q * 128:(q + 1) * 128, :],
                                                 accum_op=ALU.add),
               rd=[f"acc{q}_0", f"acc{q}_1", xk], wr=[xk])
            SY(lambda e, q=q, xb=xb: e.dma_start(out=out_d[q * 128:(q + 1) * 128, :], in_=xb[:]),
               rd=[xk], wr=[f"out{q}"])
        P.phase = ph

    def sweep_E(t, s, units=None):
        lcs = (2 * s, 2 * s + 1)
        ui = 0

        def step():
            nonlocal ui
            if units is not None and ui < len(units):
                units[ui]()
                ui += 1

        def bkey(i, j):
            return f"mb{i}{j}"

        load_mat(t, t["mb"][0][0], bkey(0, 0), 0, "eg")
        load_mat(t, t["mb"][0][1], bkey(0, 1), 0, "eu")
        load_mat(t, t["mb"][1][0], bkey(1, 0), 1, "eg")
        load_mat(t, t["mb"][1][1], bkey(1, 1), 1, "eu")
        load_Wb(t, s, 0, 0)
        load_Wb(t, s, 1, 1)
        for pair in range(17):
            c = pair % 2
            e0, e1 = 2 * pair, 2 * pair + 1
            eg0, eu0 = t["mb"][c][0], t["mb"][c][1]
            eg1, eu1 = t["mb"][1 - c][0], t["mb"][1 - c][1]
            k00, k01 = bkey(c, 0), bkey(c, 1)
            k10, k11 = bkey(1 - c, 0), bkey(1 - c, 1)
            for lc in lcs:
                P.phase = f"E{s}"
                expert_gu(t, 0, lc, e0 < NEXP, eg0, k00, eu0, k01)
                step()
                P.phase = f"E{s}"
                expert_gu(t, 1, lc, e1 < NEXP, eg1, k10, eu1, k11)
                step()
            P.phase = f"E{s}"
            load_mat(t, t["mb"][c][0], k00, e0, "ed")
            load_mat(t, t["mb"][c][1], k01, e1, "ed")
            if e0 + 2 < 34:
                load_mat(t, t["mb"][1 - c][0], k10, e0 + 2, "eg")
                load_mat(t, t["mb"][1 - c][1], k11, e0 + 2, "eu")
                load_Wb(t, s, 0, e0 + 2)
            for lc in lcs:
                P.phase = f"E{s}"
                down_pair(t, lc, pair == 0, t["mb"][c][0], k00, t["mb"][c][1], k01)
                if pair == 16:
                    emit_F(t, lc)
                step()
            if e1 + 2 < 34:
                P.phase = f"E{s}"
                load_mat(t, t["mb"][c][0], k00, e1 + 2, "eg")
                load_mat(t, t["mb"][c][1], k01, e1 + 2, "eu")
                load_Wb(t, s, 1, e1 + 2)
        while units is not None and ui < len(units):
            units[ui]()
            ui += 1

    P.phase = "hist0"
    emit_hist(0)
    for k in (0, 1):
        P.phase = f"A{k}"
        phase_A(k)
        P.phase = f"B{k}"
        for part in range(4):
            phase_B(k, range(part * 4, part * 4 + 4))
    P.phase = "CC0"
    emit_CC(0)
    Creg0, Ct0 = make_C("_c0")
    for k in (0, 1):
        P.phase = f"C{k}"
        phase_C(k, Ct0)
    Creg0.close()
    P.barrier()
    P.phase = "D0"
    emit_D(0, "_d0")
    P.barrier()

    def u_hist1():
        P.phase = "hist1"
        V(lambda e: e.memset(hcarry[:], 0.0), rd=HC + ["hcarry"], wr=["hcarry"] + HC)
        V(lambda e: e.memset(ccarry[:], 0.0), rd=CC + ["ccarry"], wr=["ccarry"] + CC)
        emit_hist(1)

    units = [u_hist1]
    units.append(lambda: (setattr(P, "phase", "A2"), phase_A(2)))
    for part in range(NST):
        for mh in range(2):
            units.append(lambda part=part, mh=mh: (setattr(P, "phase", "B2"),
                                                   phase_B(2, range(part, part + 1),
                                                           range(mh * 8, mh * 8 + 8))))
    units.append(lambda: (setattr(P, "phase", "A3"), phase_A(3)))
    for part in range(NST):
        for mh in range(2):
            units.append(lambda part=part, mh=mh: (setattr(P, "phase", "B3"),
                                                   phase_B(3, range(part, part + 1),
                                                           range(mh * 8, mh * 8 + 8))))
    units.append(lambda: (setattr(P, "phase", "CC1"), emit_CC(1)))

    Ereg0, Et0 = make_E("_e0")
    sweep_E(Et0, 0, units)
    Ereg0.close()
    P.barrier()

    Creg1, Ct1 = make_C("_c1")
    for k in (2, 3):
        P.phase = f"C{k}"
        phase_C(k, Ct1)
    Creg1.close()
    P.barrier()
    P.phase = "D1"
    emit_D(1, "_d1")
    P.barrier()
    Ereg1, Et1 = make_E("_e1")
    sweep_E(Et1, 1)
    Ereg1.close()

    P.emit("dve", lambda e: e.nop(), rd=[f"out{q}" for q in range(16)], wr=[])
    AB.close()

    sems = {}
    sem_es = ExitStack()
    sem_names = ["pe", "act", "dve", "pool"] + \
        [f"sync{i}" for i in range(Prog.NSLOT)] + [f"gdma{i}" for i in range(Prog.NSLOT)]
    for s in sem_names:
        sems[s] = sem_es.enter_context(nc.semaphore(f"sem_{s}"))
    run = P.materialize(sems)
    with nc.Block() as block:
        @block.tensor
        def _(eng):
            run(("pe",), eng, nc)

        @block.scalar
        def _(eng):
            run(("act",), eng, nc)

        @block.vector
        def _(eng):
            run(("dve",), eng, nc)

        @block.gpsimd
        def _(eng):
            run(("gdma", "pool"), eng, nc)

        @block.sync
        def _(eng):
            run(("sync",), eng, nc)
    ES.close()
    sem_es.close()
    nc.finalize()
    return nc


def prep_inputs(inputs):
    x = np.asarray(inputs["L0_attn_x"], np.float32)
    shared = {}
    shared["w_in_f"] = np.ascontiguousarray(inputs["L0_attn_w_in"], dtype=np.float32)
    shared["b_in_l"] = np.ascontiguousarray(np.asarray(inputs["L0_attn_b_in"], np.float32).reshape(32, 128).T)
    wc = np.asarray(inputs["L0_attn_w_conv"], np.float32)
    shared["wconv_l"] = np.ascontiguousarray(wc.reshape(TW, NDT, 128).transpose(2, 1, 0).reshape(128, NDT * TW))
    shared["bconv_l"] = np.ascontiguousarray(np.asarray(inputs["L0_attn_b_conv"], np.float32).reshape(NDT, 128).T)
    shared["wx_f"] = np.ascontiguousarray(inputs["L0_attn_w_x"], dtype=np.float32)
    shared["wdt_f"] = np.ascontiguousarray(inputs["L0_attn_w_dt"], dtype=np.float32)
    al = np.asarray(inputs["L0_attn_A_log"], np.float32)
    shared["alog_l"] = np.ascontiguousarray(al.reshape(NDT, 128, NST).transpose(1, 0, 2).reshape(128, NDT * NST))
    shared["d_l"] = np.ascontiguousarray(np.asarray(inputs["L0_attn_D"], np.float32).reshape(NDT, 128).T)
    shared["wout_f"] = np.ascontiguousarray(inputs["L0_attn_w_out"], dtype=np.float32)
    shared["gwT_f"] = np.ascontiguousarray(np.asarray(inputs["L0_ffn_gate_w"], np.float32).T)
    shared["gb_row"] = np.ascontiguousarray(np.asarray(inputs["L0_ffn_gate_bias"], np.float32).reshape(1, NEXP))
    shared["egT_f"] = np.ascontiguousarray(np.asarray(inputs["L0_ffn_expert_gate"], np.float32).transpose(0, 2, 1))
    shared["euT_f"] = np.ascontiguousarray(np.asarray(inputs["L0_ffn_expert_up"], np.float32).transpose(0, 2, 1))
    shared["edT_f"] = np.ascontiguousarray(np.asarray(inputs["L0_ffn_expert_down"], np.float32).transpose(0, 2, 1))
    shared["sgT_f"] = np.ascontiguousarray(np.asarray(inputs["L0_ffn_shared_gate"], np.float32).T)
    shared["suT_f"] = np.ascontiguousarray(np.asarray(inputs["L0_ffn_shared_up"], np.float32).T)
    shared["sdT_f"] = np.ascontiguousarray(np.asarray(inputs["L0_ffn_shared_down"], np.float32).T)
    shared["ident_bf"] = np.eye(128, dtype=ml_dtypes.bfloat16)
    shared["ident_f"] = np.eye(128, dtype=np.float32)

    in_maps = []
    HTOK = TOK // 2
    for c in range(NCORE):
        lo = c * HTOK
        parts = []
        for b in range(2):
            if c == 0:
                parts.append(np.concatenate([np.zeros((3, H), np.float32),
                                             x[b, lo:lo + HTOK]], axis=0))
            else:
                parts.append(x[b, lo - 3:lo + HTOK])
        m = dict(shared)
        m["x_shard"] = np.ascontiguousarray(np.concatenate(parts, axis=0))
        m["mask_in"] = np.array([[0.0 if c == 0 else 1.0]], np.float32)
        pm = np.zeros((8, 1), np.float32)
        pm[c, 0] = 1.0
        m["posmask"] = pm
        in_maps.append(m)
    return in_maps


def kernel(**inputs):
    in_maps = prep_inputs(inputs)
    nc = make_program(do_cc=True)
    res = run_bass_kernel_spmd(nc, in_maps, core_ids=list(range(NCORE)))
    out = np.empty((Bb, Lb, H), np.float32)
    HTOK = TOK // 2
    for c in range(NCORE):
        r = res.results[c]["out"]
        out[0, c * HTOK:(c + 1) * HTOK, :] = r[0:HTOK]
        out[1, c * HTOK:(c + 1) * HTOK, :] = r[HTOK:TOK]
    return out


# version 17
# speedup vs baseline: 1.0878x; 1.0878x over previous
import numpy as np
import ml_dtypes
from contextlib import ExitStack

import concourse.bass as bass
import concourse.mybir as mybir
from concourse.bass_utils import run_bass_kernel_spmd

F32 = mybir.dt.float32
BF16 = mybir.dt.bfloat16
AF = mybir.ActivationFunctionType
ALU = mybir.AluOpType
AX = mybir.AxisListType.X

Bb, Lb, H, DIN, NST, DTR, TW = 2, 8192, 1024, 2048, 16, 64, 4
NCORE = 8
TOK = 2048
CH = 512
NCHUNK = TOK // CH
NDT = DIN // 128
NKH = H // 128
NEXP, TOPK, NGRP = 32, 8, 4
RSCALE = 2.5


class Prog:
    ENGS = ("pe", "act", "dve", "pool", "sync", "gdma")
    NSLOT = 8

    def __init__(self):
        self.ops = []
        self.state = {}
        self.bar = set()
        self.dma_count = {"sync": 0, "gdma": 0}
        self.slot_last = {}
        self.phase = "setup"

    def emit(self, eng, fn, rd=(), wr=(), dma=False, inc=None, tag=""):
        idx = len(self.ops)
        op = dict(eng=eng, fn=fn, dma=dma, deps=set(), need_inc=False, inc=inc, tag=tag, phase=self.phase)
        if dma:
            slot = self.dma_count[eng] % self.NSLOT
            self.dma_count[eng] += 1
            op["sem_name"] = f"{eng}{slot}"
            op["need_inc"] = True
            prev = self.slot_last.get((eng, slot))
            if prev is not None:
                op["deps"].add(prev)
            self.slot_last[(eng, slot)] = idx
        else:
            op["sem_name"] = eng
        for b in self.bar:
            if self._xdep(b, eng, dma):
                op["deps"].add(b)
        for k in rd:
            st = self.state.get(k)
            if st is not None and st["w"] is not None and self._xdep(st["w"], eng, dma):
                op["deps"].add(st["w"])
        for k in wr:
            st = self.state.get(k)
            if st is not None:
                if st["w"] is not None and self._xdep(st["w"], eng, dma):
                    op["deps"].add(st["w"])
                for r in st["r"]:
                    if self._xdep(r, eng, dma):
                        op["deps"].add(r)
        self.ops.append(op)
        for k in rd:
            self.state.setdefault(k, {"w": None, "r": []})["r"].append(idx)
        for k in wr:
            self.state[k] = {"w": idx, "r": []}
        for d in op["deps"]:
            self.ops[d]["need_inc"] = True
        return idx

    def _xdep(self, prod_idx, eng, dma):
        p = self.ops[prod_idx]
        if p["dma"] or dma:
            return True
        if p["eng"] != eng:
            return True
        return eng != "pe"

    def barrier(self):
        last = {}
        for i, op in enumerate(self.ops):
            last[op["sem_name"]] = i
        self.bar = set(last.values())
        for i in self.bar:
            self.ops[i]["need_inc"] = True

    def materialize(self, sems):
        val = {s: 0 for s in sems}
        for op in self.ops:
            if op["need_inc"] or op["inc"] is not None:
                s = op["sem_name"]
                step = op["inc"] if op["inc"] is not None else (16 if op["dma"] else 1)
                val[s] += step
                op["sem_target"] = (s, val[s])
                op["step"] = step
            else:
                op["sem_target"] = None
        waited = {e: {} for e in self.ENGS}
        for op in self.ops:
            w = {}
            for d in op["deps"]:
                s, v = self.ops[d]["sem_target"]
                w[s] = max(w.get(s, 0), v)
            ww = []
            for s, v in w.items():
                if waited[op["eng"]].get(s, 0) < v:
                    waited[op["eng"]][s] = v
                    ww.append((s, v))
            op["waits"] = ww

        def run(eng_names, eng, nc=None):
            for op in self.ops:
                if op["eng"] not in eng_names:
                    continue
                if nc is not None:
                    with nc.named_scope(op["phase"]):
                        for s, v in op["waits"]:
                            eng.wait_ge(sems[s], v)
                        inst = op["fn"](eng)
                        if op["sem_target"] is not None:
                            inst.then_inc(sems[op["sem_name"]], op["step"])
                else:
                    for s, v in op["waits"]:
                        eng.wait_ge(sems[s], v)
                    inst = op["fn"](eng)
                    if op["sem_target"] is not None:
                        inst.then_inc(sems[op["sem_name"]], op["step"])
        return run


def make_program(do_cc=True):
    nc = bass.Bass("TRN2", target_bir_lowering=False)

    def din(name, shape, dt=F32):
        return nc.dram_tensor(name, shape, dt, kind="ExternalInput").ap()

    x_d = din("x_shard", [2 * 1027, H])
    mask_d = din("mask_in", [1, 1])
    posm_d = din("posmask", [8, 1])
    win_d = din("w_in_f", [H, 2 * DIN])
    binl_d = din("b_in_l", [128, 32])
    wconv_d = din("wconv_l", [128, NDT * TW])
    bconv_d = din("bconv_l", [128, NDT])
    wx_d = din("wx_f", [DIN, DTR + 2 * NST])
    wdt_d = din("wdt_f", [DTR, DIN])
    alog_d = din("alog_l", [128, NDT * NST])
    dl_d = din("d_l", [128, NDT])
    wout_d = din("wout_f", [DIN, H])
    gwT_d = din("gwT_f", [H, NEXP])
    gb_d = din("gb_row", [1, NEXP])
    egT_d = din("egT_f", [NEXP, H, 512])
    euT_d = din("euT_f", [NEXP, H, 512])
    edT_d = din("edT_f", [NEXP, 512, H])
    sgT_d = din("sgT_f", [H, H])
    suT_d = din("suT_f", [H, H])
    sdT_d = din("sdT_f", [H, H])
    identb_d = din("ident_bf", [128, 128], BF16)
    identf_d = din("ident_f", [128, 128], F32)

    out_d = nc.dram_tensor("out", [TOK, H], F32, kind="ExternalOutput").ap()

    def scr(name, shape, dt):
        return nc.dram_tensor(name, shape, dt, kind="Internal").ap()

    winbf_d = scr("winbf", [H, 2 * DIN], BF16)
    egbf_d = scr("egbf", [NEXP, H, 512], BF16)
    eubf_d = scr("eubf", [NEXP, H, 512], BF16)
    edbf_d = scr("edbf", [NEXP, 512, H], BF16)
    sgbf_d = scr("sgbf", [H, H], BF16)
    subf_d = scr("subf", [H, H], BF16)
    sdbf_d = scr("sdbf", [H, H], BF16)
    acc_d = scr("acc_scr", [TOK, H], F32)
    woutbf_d = scr("woutbf", [DIN, H], BF16)
    bc_d = scr("bc_scr", [NCHUNK, 32, CH], BF16)
    ub_d = scr("ub_scr", [NCHUNK, 128, NDT * CH], BF16)
    sres_d = scr("sres_scr", [NCHUNK, 128, NDT * CH], BF16)
    y_d = scr("y_scr", [NCHUNK, 128, NDT * CH], BF16)
    cums_d = scr("cums_scr", [NCHUNK, 128, NDT * CH], BF16)
    x1_d = scr("x1_scr", [TOK, H], F32)
    wt_d = scr("wt_scr", [NEXP, TOK], BF16)
    yg_d = scr("yg_scr", [NCHUNK, 128, NDT * CH], BF16)
    cc_ins = [scr(f"cc_in{s}", [128, 272], F32) for s in range(2)]
    cc_outs = [scr(f"cc_out{s}", [8 * 128, 272], F32) for s in range(2)]

    P = Prog()
    ES = ExitStack()

    def sbG(name, shape, dt):
        return ES.enter_context(nc.sbuf_tensor(name, shape, dt))

    def psG(name, shape, dt):
        return ES.enter_context(nc.psum_tensor(name, shape, dt))

    binl = sbG("binl", [128, 32], F32)
    wconv = sbG("wconv", [128, NDT * TW], F32)
    bconv = sbG("bconv", [128, NDT], F32)
    wxbf = sbG("wxbf", [128, NDT * (DTR + 2 * NST)], BF16)
    wdtbf = sbG("wdtbf", [64, DIN], BF16)
    negA = sbG("negA", [128, NDT * NST], F32)
    dl = sbG("dl", [128, NDT], F32)
    gwTbf = sbG("gwTbf", [128, NKH * NEXP], BF16)
    gbb = sbG("gbb", [128, NEXP], F32)
    identb = sbG("identb", [128, 128], BF16)
    hcarry = sbG("hcarry", [128, NDT * NST], F32)
    ccarry = sbG("ccarry", [128, NDT], F32)
    h_in = sbG("h_in", [128, NDT * NST], F32)
    maskb = sbG("maskb", [128, 1], F32)
    pmb = sbG("pmb", [128, 8], F32)
    consts = sbG("consts", [128, 8], F32)

    stat = sbG("stat", [128, 64], F32)
    tsA = [sbG(f"tsA{i}", [128, CH], F32) for i in range(2)]
    esA = [sbG("esA0", [128, CH], F32)]
    tsB = [sbG(f"tsB{i}", [128, CH], BF16) for i in range(2)]
    cbuf = [sbG(f"cbuf{i}", [128, CH], BF16) for i in range(2)]

    pb = [psG(f"pb{i}", [128, 512], F32) for i in range(7)]
    pbf = psG("pbf", [128, 1024], BF16)

    def V(fn, rd=(), wr=(), tag=""):
        P.emit("dve", fn, rd, wr, tag=tag)

    def A_(fn, rd=(), wr=(), tag=""):
        P.emit("act", fn, rd, wr, tag=tag)

    def PE(fn, rd=(), wr=(), tag=""):
        P.emit("pe", fn, rd, wr, tag=tag)

    def SY(fn, rd=(), wr=(), tag=""):
        P.emit("sync", fn, rd, wr, dma=True, tag=tag)

    def GD(fn, rd=(), wr=(), tag=""):
        P.emit("gdma", fn, rd, wr, dma=True, tag=tag)

    def PL(fn, rd=(), wr=(), inc=None, tag=""):
        P.emit("pool", fn, rd, wr, inc=inc, tag=tag)

    HC = [f"hcar{c}" for c in range(NDT * NST)]
    CC = [f"ccar{m}" for m in range(NDT)]

    V(lambda e: e.memset(consts[:, 0:1], 1e-5), wr=["consts"])
    V(lambda e: e.memset(consts[:, 1:2], -0.5), wr=["consts"])
    V(lambda e: e.memset(consts[:, 2:3], 1.0), wr=["consts"])
    V(lambda e: e.memset(hcarry[:], 0.0), wr=["hcarry"] + HC)
    V(lambda e: e.memset(ccarry[:], 0.0), wr=["ccarry"] + CC)
    V(lambda e: e.memset(h_in[:], 0.0), wr=["h_in"])

    SY(lambda e: e.dma_start(out=binl[:], in_=binl_d[:]), wr=["binl"])
    SY(lambda e: e.dma_start(out=wconv[:], in_=wconv_d[:]), wr=["wconv"])
    SY(lambda e: e.dma_start(out=bconv[:], in_=bconv_d[:]), wr=["bconv"])
    SY(lambda e: e.dma_start(out=dl[:], in_=dl_d[:]), wr=["dl"])
    SY(lambda e: e.dma_start(out=gbb[:], in_=gb_d.partition_broadcast(128)), wr=["gbb"])
    SY(lambda e: e.dma_start(out=identb[:], in_=identb_d[:]), wr=["identb"])
    GD(lambda e: e.dma_start(out=maskb[:], in_=mask_d.partition_broadcast(128)), wr=["maskb"])
    GD(lambda e: e.dma_start(out=pmb[:], in_=posm_d.rearrange("k o -> o k").partition_broadcast(128)), wr=["pmb"])
    GD(lambda e: e.dma_start(out=wxbf[:].rearrange("p (k m) -> p k m", m=DTR + 2 * NST), in_=wx_d.rearrange("(k p) m -> p k m", p=128)), wr=["wxbf"])
    GD(lambda e: e.dma_start(out=wdtbf[:], in_=wdt_d[:]), wr=["wdtbf"])
    GD(lambda e: e.dma_start(out=gwTbf[:].rearrange("p (k m) -> p k m", m=NEXP), in_=gwT_d.rearrange("(k p) m -> p k m", p=128)), wr=["gwTbf"])
    GD(lambda e: e.dma_start(out=winbf_d[:], in_=win_d[:]), wr=["winbf"])
    GD(lambda e: e.dma_start(out=woutbf_d[:], in_=wout_d[:]), wr=["woutbf"])
    for eb in range(4):
        GD(lambda e, eb=eb: e.dma_start(out=egbf_d[eb * 8:(eb + 1) * 8], in_=egT_d[eb * 8:(eb + 1) * 8]), wr=["egbf"])
        GD(lambda e, eb=eb: e.dma_start(out=eubf_d[eb * 8:(eb + 1) * 8], in_=euT_d[eb * 8:(eb + 1) * 8]), wr=["eubf"])
        GD(lambda e, eb=eb: e.dma_start(out=edbf_d[eb * 8:(eb + 1) * 8], in_=edT_d[eb * 8:(eb + 1) * 8]), wr=["edbf"])
    GD(lambda e: e.dma_start(out=sgbf_d[:], in_=sgT_d[:]), wr=["sgbf"])
    GD(lambda e: e.dma_start(out=subf_d[:], in_=suT_d[:]), wr=["subf"])
    GD(lambda e: e.dma_start(out=sdbf_d[:], in_=sdT_d[:]), wr=["sdbf"])

    lnh = sbG("lnh", [128, NDT * NST], F32)
    sgnh = sbG("sgnh", [128, NDT * NST], F32)
    e2bf = [sbG(f"e2bf{i}", [128, CH], BF16) for i in range(2)]
    agr = [sbG("agr0", [128, 272], F32)]
    V(lambda e: e.memset(consts[:, 4:5], 1e-38), rd=["consts"], wr=["consts"])

    HT = TOK // 2
    xn2T = sbG("xn2T", [128, NKH * HT], BF16)

    AB = ExitStack()

    def sbA(name, shape, dt):
        return AB.enter_context(nc.sbuf_tensor(name, shape, dt))

    alog_sb = sbA("alog_sb", [128, NDT * NST], F32)
    xc2 = [sbA(f"xc2_{i}", [128, 520], BF16) for i in range(2)]
    tailb = sbA("tailb", [128, NDT * 4], BF16)
    xf = [sbA(f"xf_{i}", [128, H], F32) for i in range(2)]
    xn = [sbA(f"xn_{i}", [128, H], BF16) for i in range(2)]
    xnT = sbA("xnT", [128, NKH * CH], BF16)
    wbufs = [sbA(f"wbuf{i}", [128, NKH * 128], BF16) for i in range(2)]
    sresT = [sbA(f"sresT{i}", [128, CH], BF16) for i in range(2)]
    ubfT = sbA("ubfT", [128, NDT * CH], BF16)
    dd = sbA("dd", [128, 2 * NDT * CH], BF16)
    deltaT = dd[:, 0:NDT * CH]
    dubT = dd[:, NDT * CH:2 * NDT * CH]
    woutali = dd
    dtbfT = sbA("dtbfT", [64, CH], BF16)
    bcs = sbA("bcs", [32, CH], BF16)
    y_sb = ubfT
    cums_sb = sbA("cums_sb", [128, CH], BF16)
    a2 = [sbA(f"a2_{i}", [128, CH], F32) for i in range(2)]
    bu2 = [sbA(f"bu2_{i}", [128, CH], BF16) for i in range(2)]
    hs2 = [sbA(f"hs2_{i}", [128, CH], BF16) for i in range(2)]
    bbuf = [sbA(f"bbuf{i}", [128, CH], BF16) for i in range(2)]
    xh_s, xhn = xf[0], xn[0]
    xhT = sbA("xhT", [128, NKH * 3], BF16)

    SY(lambda e: e.dma_start(out=alog_sb[:], in_=alog_d[:]), wr=["alog"])
    A_(lambda e: e.activation(out=negA[:], in_=alog_sb[:], func=AF.Exp), rd=["alog"], wr=["negA"])
    V(lambda e: e.tensor_scalar(out=negA[:], in0=negA[:], scalar1=-1.0, scalar2=None, op0=ALU.mult),
      rd=["negA"], wr=["negA"])

    def emit_ln(src_ap, dst_ap, nrow, src_key, dst_key, c0=0):
        A_(lambda e: e.activation(out=dst_ap, in_=src_ap, func=AF.Square,
                                  accum_out=stat[0:nrow, c0 + 0:c0 + 1]), rd=[src_key, dst_key], wr=[dst_key, f"st0_{c0}"])
        V(lambda e: e.tensor_reduce(out=stat[0:nrow, c0 + 1:c0 + 2], in_=src_ap, axis=AX, op=ALU.add),
          rd=[src_key], wr=[f"st1_{c0}"])
        V(lambda e: e.tensor_scalar(out=stat[0:nrow, c0 + 2:c0 + 3], in0=stat[0:nrow, c0 + 1:c0 + 2], scalar1=1.0 / H,
                                    scalar2=None, op0=ALU.mult), rd=[f"st1_{c0}"], wr=[f"st2_{c0}"])
        V(lambda e: e.tensor_tensor(out=stat[0:nrow, c0 + 3:c0 + 4], in0=stat[0:nrow, c0 + 2:c0 + 3],
                                    in1=stat[0:nrow, c0 + 2:c0 + 3], op=ALU.mult), rd=[f"st2_{c0}"], wr=[f"st3_{c0}"])
        V(lambda e: e.scalar_tensor_tensor(out=stat[0:nrow, c0 + 4:c0 + 5], in0=stat[0:nrow, c0 + 0:c0 + 1],
                                           scalar=1.0 / H, in1=stat[0:nrow, c0 + 3:c0 + 4],
                                           op0=ALU.mult, op1=ALU.subtract),
          rd=[f"st0_{c0}", f"st3_{c0}"], wr=[f"st4_{c0}"])
        A_(lambda e: e.activation(out=stat[0:nrow, c0 + 5:c0 + 6], in_=stat[0:nrow, c0 + 4:c0 + 5], func=AF.Ln,
                                  bias=consts[0:nrow, 0:1]), rd=[f"st4_{c0}", "consts"], wr=[f"st5_{c0}"])
        A_(lambda e: e.activation(out=stat[0:nrow, c0 + 6:c0 + 7], in_=stat[0:nrow, c0 + 5:c0 + 6], func=AF.Exp,
                                  scale=consts[0:nrow, 1:2]), rd=[f"st5_{c0}", "consts"], wr=[f"st6_{c0}"])
        V(lambda e: e.tensor_tensor(out=stat[0:nrow, c0 + 7:c0 + 8], in0=stat[0:nrow, c0 + 2:c0 + 3],
                                    in1=stat[0:nrow, c0 + 6:c0 + 7], op=ALU.mult), rd=[f"st2_{c0}", f"st6_{c0}"], wr=[f"st7_{c0}"])
        V(lambda e: e.tensor_scalar(out=dst_ap, in0=src_ap, scalar1=stat[0:nrow, c0 + 6:c0 + 7],
                                    scalar2=stat[0:nrow, c0 + 7:c0 + 8], op0=ALU.mult, op1=ALU.subtract),
          rd=[src_key, f"st6_{c0}", f"st7_{c0}"], wr=[dst_key])

    def xrow(k):
        return (k // 2) * 1027 + 3 + (k % 2) * CH

    def emit_hist(s):
        SY(lambda e: e.dma_start(out=xh_s[0:3, :], in_=x_d[s * 1027:s * 1027 + 3, :]),
           rd=["xf_0"], wr=["xf_0"])
        emit_ln(xh_s[0:3, :], xhn[0:3, :], 3, "xf_0", "xn_0", c0=32)
        for kk in range(NKH):
            PE(lambda e, kk=kk: e.transpose(pbf[:, kk * 128:kk * 128 + 3],
                                            xhn[0:3, kk * 128:(kk + 1) * 128], identb[0:3, 0:3]),
               rd=["xn_0", "identb"], wr=["PBF"])
            A_(lambda e, kk=kk: e.copy(out=xhT[:, kk * 3:(kk + 1) * 3], in_=pbf[:, kk * 128:kk * 128 + 3]),
               rd=["PBF", "xhT"], wr=["xhT"])
        for m in range(NDT):
            wb = wbufs[m % 2]
            SY(lambda e, m=m, wb=wb: e.dma_start(
                out=wb[:], in_=winbf_d.rearrange("(k p) f -> p k f", p=128)[:, :, m * 128:(m + 1) * 128]),
               rd=["winbf", f"wb{m % 2}"], wr=[f"wb{m % 2}"])
            for kk in range(NKH):
                PE(lambda e, m=m, kk=kk, wb=wb: e.matmul(
                    pb[5][0:128, 0:3], wb[:, kk * 128:(kk + 1) * 128], xhT[:, kk * 3:(kk + 1) * 3],
                    start=(kk == 0), stop=(kk == NKH - 1)),
                   rd=[f"wb{m % 2}", "xhT"], wr=["PB5"])
            A_(lambda e, m=m: e.activation(out=tsA[0][:, 0:3], in_=pb[5][0:128, 0:3], func=AF.Identity,
                                           bias=binl[:, m:m + 1]), rd=["PB5", "binl", "tsA0"], wr=["tsA0"])
            V(lambda e, m=m: e.tensor_scalar(out=tailb[:, m * 4:m * 4 + 3], in0=tsA[0][:, 0:3],
                                             scalar1=maskb[:], scalar2=None, op0=ALU.mult),
              rd=["tsA0", "maskb", f"xch{m}"], wr=[f"xch{m}"])

    def phase_A(k):
        for p in range(4):
            xfp, xfk = xf[p % 2], f"xf_{p % 2}"
            xnp, xnk = xn[p % 2], f"xn_{p % 2}"
            SY(lambda e, p=p, xfp=xfp: e.dma_start(out=xfp[:],
                                                   in_=x_d[xrow(k) + p * 128:xrow(k) + (p + 1) * 128, :]),
               rd=[xfk], wr=[xfk], tag=f"ldx{k}")
            emit_ln(xfp[:], xnp[:], 128, xfk, xnk, c0=(p % 2) * 8)
            for kk in range(NKH):
                PE(lambda e, kk=kk, xnp=xnp: e.transpose(
                    pbf[:, (kk % 8) * 128:(kk % 8) * 128 + 128],
                    xnp[:, kk * 128:(kk + 1) * 128], identb[:]),
                   rd=[xnk, "identb"], wr=["PBF"])
                A_(lambda e, p=p, kk=kk: e.copy(
                    out=xnT[:, kk * CH + p * 128:kk * CH + (p + 1) * 128],
                    in_=pbf[:, (kk % 8) * 128:(kk % 8) * 128 + 128]),
                   rd=["PBF"], wr=[f"xnT{kk}"])
        for m in range(2 * NDT):
            wb = wbufs[m % 2]
            SY(lambda e, m=m, wb=wb: e.dma_start(
                out=wb[:], in_=winbf_d.rearrange("(k p) f -> p k f", p=128)[:, :, m * 128:(m + 1) * 128]),
               rd=["winbf"], wr=[f"wb{m % 2}"])
            bank, bkey = pb[5 + (m % 2)], f"PB{5 + (m % 2)}"
            for kk in range(NKH):
                PE(lambda e, m=m, kk=kk, wb=wb, bank=bank: e.matmul(
                    bank[:], wb[:, kk * 128:(kk + 1) * 128], xnT[:, kk * CH:(kk + 1) * CH],
                    start=(kk == 0), stop=(kk == NKH - 1)),
                   rd=[f"wb{m % 2}", f"xnT{kk}"], wr=[bkey])
            if m < NDT:
                xcp, xck = xc2[m % 2], f"xc2_{m % 2}"
                A_(lambda e, m=m, bank=bank, xcp=xcp: e.activation(
                    out=xcp[:, 3:3 + CH], in_=bank[:], func=AF.Identity,
                    bias=binl[:, m:m + 1]), rd=[bkey, "binl", xck], wr=[xck])
                V(lambda e, m=m, xcp=xcp: e.tensor_copy(out=xcp[:, 0:3],
                                                        in_=tailb[:, m * 4:m * 4 + 3]),
                  rd=[f"xch{m}", xck], wr=[xck])
                cs, ck = tsA[m % 2], f"tsA{m % 2}"
                V(lambda e, m=m, cs=cs, xcp=xcp: e.tensor_scalar(
                    out=cs[:], in0=xcp[:, 0:CH], scalar1=wconv[:, m * TW:m * TW + 1],
                    scalar2=None, op0=ALU.mult), rd=[xck, "wconv", ck], wr=[ck])
                for kc in range(1, TW):
                    V(lambda e, m=m, kc=kc, cs=cs, xcp=xcp: e.scalar_tensor_tensor(
                        out=cs[:], in0=xcp[:, kc:kc + CH],
                        scalar=wconv[:, m * TW + kc:m * TW + kc + 1], in1=cs[:],
                        op0=ALU.mult, op1=ALU.add),
                      rd=[xck, "wconv", ck], wr=[ck])
                A_(lambda e, m=m, cs=cs: e.activation(out=ubfT[:, m * CH:(m + 1) * CH], in_=cs[:],
                                                      func=AF.Silu, bias=bconv[:, m:m + 1]),
                   rd=[ck, "bconv", f"ubf{m}", f"y{m}"], wr=[f"ubf{m}", f"y{m}"])
                V(lambda e, m=m, xcp=xcp: e.tensor_copy(out=tailb[:, m * 4:m * 4 + 3],
                                                        in_=xcp[:, CH:CH + 3]),
                  rd=[xck], wr=[f"xch{m}"])
            else:
                mm = m - NDT
                srp, srk = sresT[mm % 2], f"sres{mm % 2}"
                A_(lambda e, m=m, bank=bank, srp=srp: e.activation(
                    out=srp[:], in_=bank[:], func=AF.Silu,
                    bias=binl[:, m:m + 1]), rd=[bkey, "binl", srk], wr=[srk])
                SY(lambda e, mm=mm, srp=srp: e.dma_start(out=sres_d[k, :, mm * CH:(mm + 1) * CH],
                                                         in_=srp[:]),
                   rd=[srk], wr=[f"sresd{k}"])
        SY(lambda e: e.dma_start(out=ub_d[k], in_=ubfT[:]),
           rd=[f"ubf{m}" for m in range(NDT)], wr=[f"ubd{k}"])
        for kk in range(NDT):
            PE(lambda e, kk=kk: e.matmul(pb[6][0:96, 0:CH], wxbf[:, kk * 96:(kk + 1) * 96],
                                         ubfT[:, kk * CH:(kk + 1) * CH],
                                         start=(kk == 0), stop=(kk == NDT - 1)),
               rd=[f"ubf{kk}", "wxbf"], wr=["PB6"])
        A_(lambda e: e.copy(out=dtbfT[:], in_=pb[6][0:64, 0:CH]), rd=["PB6", "dtbf"], wr=["dtbf"])
        A_(lambda e: e.copy(out=bcs[:], in_=pb[6][64:96, 0:CH]), rd=["PB6", "bcs"], wr=["bcs"])
        GD(lambda e: e.dma_start(out=bc_d[k], in_=bcs[:]), rd=["bcs"], wr=[f"bcd{k}"])
        for m in range(NDT):
            bank, bkey = pb[5 + (m % 2)], f"PB{5 + (m % 2)}"
            PE(lambda e, m=m, bank=bank: e.matmul(
                bank[:], wdtbf[:, m * 128:(m + 1) * 128],
                dtbfT[:], start=True, stop=True),
               rd=["dtbf", "wdtbf"], wr=[bkey])
            ej = esA[0]
            ejk = "esA0"
            A_(lambda e, bank=bank, ej=ej: e.activation(out=ej[:], in_=bank[:], func=AF.Exp),
               rd=[bkey, ejk], wr=[ejk])
            A_(lambda e, m=m, ej=ej: e.activation(out=deltaT[:, m * CH:(m + 1) * CH], in_=ej[:],
                                           func=AF.Ln, bias=consts[:, 2:3]),
               rd=[ejk, "consts", "woutsb"], wr=[f"del{m}"])
            V(lambda e, m=m: e.tensor_tensor(out=dubT[:, m * CH:(m + 1) * CH],
                                             in0=deltaT[:, m * CH:(m + 1) * CH],
                                             in1=ubfT[:, m * CH:(m + 1) * CH], op=ALU.mult),
              rd=[f"del{m}", f"ubf{m}", "woutsb"], wr=[f"dub{m}"])

    def phase_B(k, n_range, m_range=None):
        if m_range is None:
            m_range = range(NDT)
        do_cums = n_range.start == 0 and m_range.start == 0
        for m in (range(NDT) if do_cums else ()):
            V(lambda e, m=m: e.tensor_tensor_scan(out=cums_sb[:], data0=consts[:, 2:3].broadcast_to((128, CH)),
                                                  data1=deltaT[:, m * CH:(m + 1) * CH],
                                                  initial=ccarry[:, m:m + 1],
                                                  op0=ALU.mult, op1=ALU.add),
              rd=[f"del{m}", "consts", f"ccar{m}"], wr=["cums"])
            V(lambda e, m=m: e.tensor_copy(out=ccarry[:, m:m + 1], in_=cums_sb[:, CH - 1:CH]),
              rd=["cums"], wr=[f"ccar{m}"])
            SY(lambda e, m=m: e.dma_start(out=cums_d[k, :, m * CH:(m + 1) * CH], in_=cums_sb[:]),
               rd=["cums"], wr=[f"cumsd{k}"])
        def load_bb(n):
            bB, bC = bbuf[n % 2], cbuf[n % 2]
            SY(lambda e, n=n, bB=bB: e.dma_start(
                out=bB[:], in_=bc_d[k, n:n + 1, :].partition_broadcast(128)),
               rd=[f"bcd{k}", f"bb{n % 2}"], wr=[f"bb{n % 2}"])
            SY(lambda e, n=n, bC=bC: e.dma_start(
                out=bC[:], in_=bc_d[k, NST + n:NST + n + 1, :].partition_broadcast(128)),
               rd=[f"bcd{k}", f"cb{n % 2}"], wr=[f"cb{n % 2}"])
        if do_cums:
            load_bb(0)
        for n in n_range:
            bB, bC = bbuf[n % 2], cbuf[n % 2]
            if n + 1 < NST and m_range.start == 0:
                load_bb(n + 1)
            for m in m_range:
                col = m * NST + n
                j = (n * NDT + m) % 2
                aj, bj, hj, tj = a2[j], bu2[j], hs2[j], tsB[j]
                ak, bk2, hk, tk = f"a2_{j}", f"bu2_{j}", f"hs2_{j}", f"tsB{j}"
                A_(lambda e, m=m, col=col, aj=aj: e.activation(
                    out=aj[:], in_=deltaT[:, m * CH:(m + 1) * CH], func=AF.Exp,
                    scale=negA[:, col:col + 1]), rd=[f"del{m}", "negA", ak], wr=[ak])
                V(lambda e, m=m, bB=bB, bj=bj: e.tensor_tensor(
                    out=bj[:], in0=dubT[:, m * CH:(m + 1) * CH], in1=bB[:], op=ALU.mult),
                  rd=[f"dub{m}", f"bb{n % 2}", bk2], wr=[bk2])
                V(lambda e, col=col, aj=aj, bj=bj, hj=hj: e.tensor_tensor_scan(
                    out=hj[:], data0=aj[:], data1=bj[:],
                    initial=hcarry[:, col:col + 1], op0=ALU.mult, op1=ALU.add),
                  rd=[ak, bk2, f"hcar{col}", hk], wr=[hk])
                V(lambda e, col=col, hj=hj: e.tensor_copy(out=hcarry[:, col:col + 1],
                                                          in_=hj[:, CH - 1:CH]),
                  rd=[hk], wr=[f"hcar{col}"])
                V(lambda e, bC=bC, hj=hj, tj=tj: e.tensor_tensor(out=tj[:], in0=hj[:], in1=bC[:],
                                                                 op=ALU.mult),
                  rd=[hk, f"cb{n % 2}", tk], wr=[tk])
                if n == 0:
                    V(lambda e, m=m, tj=tj: e.tensor_copy(out=y_sb[:, m * CH:(m + 1) * CH], in_=tj[:]),
                      rd=[tk, f"y{m}", f"ubf{m}"], wr=[f"y{m}", f"ubf{m}"])
                else:
                    V(lambda e, m=m, tj=tj: e.tensor_tensor(out=y_sb[:, m * CH:(m + 1) * CH],
                                                            in0=y_sb[:, m * CH:(m + 1) * CH],
                                                            in1=tj[:], op=ALU.add),
                      rd=[tk, f"y{m}", f"ubf{m}"], wr=[f"y{m}", f"ubf{m}"])
        if n_range.stop == NST and m_range.stop == NDT:
            SY(lambda e: e.dma_start(out=y_d[k], in_=y_sb[:]),
               rd=[f"y{m}" for m in range(NDT)], wr=[f"yd{k}"])

    def emit_CC(s):
        Hc = a2[0][:, 0:256]
        Pt = a2[1][:, 0:256]
        ci, co = cc_ins[s], cc_outs[s]
        SY(lambda e: e.dma_start(out=ci[:, 0:256], in_=hcarry[:]), rd=HC + ["ccin"], wr=["ccin"])
        SY(lambda e: e.dma_start(out=ci[:, 256:272], in_=ccarry[:]), rd=CC + ["ccin"], wr=["ccin"])
        if do_cc:
            PL(lambda e: e.collective_compute(
                "AllGather", ALU.bypass, replica_groups=[[0, 1, 2, 3, 4, 5, 6, 7]],
                ins=[ci[:]], outs=[co[:]]), rd=["ccin"], wr=["ccout"], inc=1)
        else:
            for r in range(8):
                GD(lambda e, r=r: e.dma_start(out=co[r * 128:(r + 1) * 128, :], in_=ci[:]),
                   rd=["ccin"], wr=["ccout"])
        def load_agr(r):
            ag, agk = agr[0], "agr0"
            SY(lambda e, r=r, ag=ag: e.dma_start(out=ag[:], in_=co[r * 128:(r + 1) * 128, :]),
               rd=["ccout", agk], wr=[agk])
        load_agr(0)
        ag0 = agr[0]
        V(lambda e, ag0=ag0: e.tensor_copy(out=Hc, in_=ag0[:, 0:256]), rd=["agr0", "Hc", "a2_0"], wr=["Hc", "a2_0"])
        V(lambda e: e.tensor_scalar(out=h_in[:], in0=Hc, scalar1=pmb[:, 1:2], scalar2=None,
                                    op0=ALU.mult), rd=["Hc", "pmb", "h_in"], wr=["h_in"])
        for r in range(1, 7):
            ag, agk = agr[0], "agr0"
            load_agr(r)
            V(lambda e, ag=ag: e.tensor_tensor(
                out=Pt.rearrange("p (d n) -> p d n", n=NST),
                in0=negA[:].rearrange("p (d n) -> p d n", n=NST),
                in1=ag[:, 256:272].unsqueeze(2).broadcast_to((128, NDT, NST)),
                op=ALU.mult), rd=[agk, "negA", "Pt", "a2_1"], wr=["Pt", "a2_1"])
            A_(lambda e: e.activation(out=Pt, in_=Pt, func=AF.Exp), rd=["Pt", "a2_1"], wr=["Pt", "a2_1"])
            V(lambda e: e.tensor_tensor(out=Hc, in0=Pt, in1=Hc, op=ALU.mult),
              rd=["Pt", "Hc", "a2_0"], wr=["Hc", "a2_0"])
            V(lambda e, ag=ag: e.tensor_tensor(out=Hc, in0=Hc, in1=ag[:, 0:256],
                                               op=ALU.add), rd=["Hc", agk, "a2_0"], wr=["Hc", "a2_0"])
            V(lambda e, r=r: e.scalar_tensor_tensor(out=h_in[:], in0=Hc, scalar=pmb[:, r + 1:r + 2],
                                                    in1=h_in[:], op0=ALU.mult, op1=ALU.add),
              rd=["Hc", "pmb", "h_in"], wr=["h_in"])
        A_(lambda e: e.activation(out=lnh[:], in_=h_in[:], func=AF.Abs), rd=["h_in", "lnh"], wr=["lnh"])
        A_(lambda e: e.activation(out=sgnh[:], in_=h_in[:], func=AF.Sign), rd=["h_in", "sgnh"], wr=["sgnh"])
        A_(lambda e: e.activation(out=lnh[:], in_=lnh[:], func=AF.Ln, bias=consts[:, 4:5]),
           rd=["lnh", "consts"], wr=["lnh"])


    def make_C(sfx):
        Creg = ExitStack()

        def sbC(name, shape, dt):
            return Creg.enter_context(nc.sbuf_tensor(name + sfx, shape, dt))

        t = {}
        t["woutbf2"] = woutali
        t["Cball"] = sbC("Cball", [128, NST * CH], BF16)
        t["cum2"] = [sbC(f"cum2_{i}", [128, CH], BF16) for i in range(2)]
        t["ub2"] = [sbC(f"ub2_{i}", [128, CH], BF16) for i in range(2)]
        t["sr2"] = [sbC(f"sr2_{i}", [128, CH], BF16) for i in range(2)]
        t["yCm"] = [sbC(f"yCm_{i}", [128, CH], F32) for i in range(2)]
        t["ygf"] = sbC("ygf", [128, NDT * CH], BF16)
        t["xfl"] = [sbC(f"xfl_{i}", [128, H], F32) for i in range(2)]
        t["xn2r"] = [sbC(f"xn2r_{i}", [128, H], BF16) for i in range(2)]
        GD(lambda e: e.dma_start(out=t["woutbf2"][:].rearrange("p (k f) -> p k f", f=H),
                                 in_=woutbf_d.rearrange("(k p) f -> p k f", p=128)),
           rd=["woutbf", "woutsb"],
           wr=["woutsb"] + [f"del{m}" for m in range(NDT)] + [f"dub{m}" for m in range(NDT)])
        return Creg, t

    def phase_C(k, t):
        SY(lambda e: e.dma_start(
            out=t["Cball"][:].rearrange("p (n l) -> p n l", l=CH),
            in_=bc_d[k, NST:2 * NST, :].unsqueeze(0).partition_broadcast(128)),
           rd=[f"bcd{k}", "Cball"], wr=["Cball"])
        for m in range(NDT):
            jm = m % 2
            cum, cumk = t["cum2"][jm], f"cum2_{jm}"
            ub, ubk = t["ub2"][jm], f"ub2_{jm}"
            sr, srk = t["sr2"][jm], f"sr2_{jm}"
            yc, yck = t["yCm"][jm], f"yCm_{jm}"
            SY(lambda e, m=m, cum=cum: e.dma_start(out=cum[:], in_=cums_d[k, :, m * CH:(m + 1) * CH]),
               rd=[f"cumsd{k}", cumk], wr=[cumk])
            SY(lambda e, m=m, ub=ub: e.dma_start(out=ub[:], in_=ub_d[k, :, m * CH:(m + 1) * CH]),
               rd=[f"ubd{k}", ubk], wr=[ubk])
            SY(lambda e, m=m, sr=sr: e.dma_start(out=sr[:], in_=sres_d[k, :, m * CH:(m + 1) * CH]),
               rd=[f"sresd{k}", srk], wr=[srk])
            GD(lambda e, m=m, yc=yc: e.dma_start(out=yc[:], in_=y_d[k, :, m * CH:(m + 1) * CH]),
               rd=[f"yd{k}", yck], wr=[yck])
            for n in range(NST):
                col = m * NST + n
                j = (m * NST + n) % 2
                ej, tj = e2bf[j], tsB[j]
                ek, tk = f"e2bf{j}", f"tsB{j}"
                A_(lambda e, col=col, ej=ej, cum=cum: e.activation(
                    out=ej[:], in_=cum[:], func=AF.Exp,
                    scale=negA[:, col:col + 1], bias=lnh[:, col:col + 1]),
                   rd=[cumk, "negA", "lnh", ek], wr=[ek])
                V(lambda e, n=n, ej=ej, tj=tj: e.tensor_tensor(
                    out=tj[:], in0=ej[:], in1=t["Cball"][:, n * CH:(n + 1) * CH], op=ALU.mult),
                  rd=[ek, "Cball", tk], wr=[tk])
                V(lambda e, col=col, tj=tj, yc=yc: e.scalar_tensor_tensor(
                    out=yc[:], in0=tj[:], scalar=sgnh[:, col:col + 1],
                    in1=yc[:], op0=ALU.mult, op1=ALU.add),
                  rd=[tk, "sgnh", yck], wr=[yck])
            V(lambda e, m=m, ub=ub, yc=yc: e.scalar_tensor_tensor(
                out=yc[:], in0=ub[:], scalar=dl[:, m:m + 1], in1=yc[:],
                op0=ALU.mult, op1=ALU.add), rd=[ubk, "dl", yck], wr=[yck])
            V(lambda e, m=m, sr=sr, yc=yc: e.tensor_tensor(
                out=t["ygf"][:, m * CH:(m + 1) * CH], in0=yc[:], in1=sr[:], op=ALU.mult),
              rd=[yck, srk, f"yg{m}"], wr=[f"yg{m}"])
        for ls in range(4):
            jl = ls % 2
            xfl, xfk = t["xfl"][jl], f"xfl_{jl}"
            xnr, xnk = t["xn2r"][jl], f"xn2r_{jl}"
            SY(lambda e, ls=ls, xfl=xfl: e.dma_start(
                out=xfl[:], in_=x_d[xrow(k) + ls * 128:xrow(k) + (ls + 1) * 128, :]),
               rd=[xfk], wr=[xfk])
            for hh in range(2):
                bank, bkey = pb[(ls * 2 + hh) % 4], f"PB{(ls * 2 + hh) % 4}"
                for m in range(NDT):
                    PE(lambda e, ls=ls, hh=hh, m=m, bank=bank: e.matmul(
                        bank[:], t["ygf"][:, m * CH + ls * 128:m * CH + (ls + 1) * 128],
                        t["woutbf2"][:, m * H + hh * 512:m * H + (hh + 1) * 512],
                        start=(m == 0), stop=(m == NDT - 1)),
                       rd=[f"yg{m}", "woutsb"], wr=[bkey])
                V(lambda e, ls=ls, hh=hh, bank=bank, xfl=xfl: e.tensor_tensor(
                    out=xfl[:, hh * 512:(hh + 1) * 512],
                    in0=xfl[:, hh * 512:(hh + 1) * 512],
                    in1=bank[:], op=ALU.add), rd=[bkey, xfk], wr=[xfk])
            SY(lambda e, ls=ls, xfl=xfl: e.dma_start(
                out=x1_d[k * CH + ls * 128:k * CH + (ls + 1) * 128, :], in_=xfl[:]),
               rd=[xfk], wr=[f"x1d{k}"])
            emit_ln(xfl[:], xnr[:], 128, xfk, xnk, c0=jl * 8)
            for kk in range(NKH):
                PE(lambda e, ls=ls, kk=kk, xnr=xnr: e.transpose(
                    pbf[:, ls * 128:(ls + 1) * 128],
                    xnr[:, kk * 128:(kk + 1) * 128], identb[:]),
                   rd=[xnk, "identb"], wr=["PBF"])
                A_(lambda e, ls=ls, kk=kk: e.copy(
                    out=xn2T[:, kk * HT + (k % 2) * CH + ls * 128:kk * HT + (k % 2) * CH + (ls + 1) * 128],
                    in_=pbf[:, ls * 128:(ls + 1) * 128]),
                   rd=["PBF"], wr=[f"x2T{kk}"])

    NQ = 8

    def emit_D(s, sfx):
        Dreg = ExitStack()

        def sbD(name, shape, dt):
            return Dreg.enter_context(nc.sbuf_tensor(name + sfx, shape, dt))

        scores = sbD("scores", [128, NQ * NEXP], F32)
        choice = sbD("choice", [128, NQ * NEXP], F32)
        msk = sbD("msk", [128, NQ * NEXP], F32)
        Wt = sbD("Wt", [128, NQ * NEXP], F32)
        eqs = sbD("eqs", [128, NQ * NEXP], F32)
        g1 = sbD("g1", [128, NQ * NGRP], F32)
        g2 = sbD("g2", [128, NQ * NGRP], F32)
        gs = sbD("gs", [128, NQ * NGRP], F32)
        thr = sbD("thr", [128, NQ], F32)
        Wtb = sbD("Wtb", [128, NQ * NEXP], BF16)
        WT = sbD("WT", [32, NQ * 128], BF16)

        def r3(ap):
            return ap.rearrange("p (q e) -> p q e", e=NEXP)

        def r3g(ap):
            return ap.rearrange("p (q g) -> p q g", g=NGRP)

        def r4(ap):
            return ap.rearrange("p (q e) -> p q e", e=NEXP).rearrange(
                "p q (g j) -> p q g j", j=NEXP // NGRP)

        def bq(ap, last):
            return ap[:, 0:NQ].unsqueeze(2).broadcast_to((128, NQ, last))

        def bqg(ap):
            return ap.rearrange("p (q g) -> p q g", g=NGRP).unsqueeze(3).broadcast_to(
                (128, NQ, NGRP, NEXP // NGRP))

        for ql in range(NQ):
            q = s * NQ + ql
            bank, bkey = pb[ql % 2], f"PB{ql % 2}"
            for kk in range(NKH):
                PE(lambda e, ql=ql, kk=kk, bank=bank: e.matmul(
                    bank[:, 0:NEXP], xn2T[:, kk * HT + ql * 128:kk * HT + (ql + 1) * 128],
                    gwTbf[:, kk * NEXP:(kk + 1) * NEXP], start=(kk == 0), stop=(kk == NKH - 1)),
                   rd=[f"x2T{kk}", "gwTbf"], wr=[bkey])
            A_(lambda e, ql=ql, bank=bank: e.activation(
                out=scores[:, ql * NEXP:(ql + 1) * NEXP], in_=bank[:, 0:NEXP], func=AF.Sigmoid),
               rd=[bkey, "scores"], wr=["scores"])
        V(lambda e: e.tensor_tensor(out=r3(choice), in0=r3(scores),
                                    in1=gbb[:].unsqueeze(1).broadcast_to((128, NQ, NEXP)),
                                    op=ALU.add), rd=["scores", "gbb", "choice"], wr=["choice"])
        V(lambda e: e.tensor_reduce(out=r3g(g1), in_=r4(choice), axis=AX, op=ALU.max),
          rd=["choice", "g1"], wr=["g1"])
        V(lambda e: e.tensor_tensor(out=r4(eqs), in0=r4(choice), in1=bqg(g1), op=ALU.is_ge),
          rd=["choice", "g1", "eqs"], wr=["eqs"])
        V(lambda e: e.scalar_tensor_tensor(out=r4(msk), in0=r4(eqs), scalar=-1e30, in1=r4(choice),
                                           op0=ALU.mult, op1=ALU.add), rd=["eqs", "choice", "msk"], wr=["msk"])
        V(lambda e: e.tensor_reduce(out=r3g(g2), in_=r4(msk), axis=AX, op=ALU.max),
          rd=["msk", "g2"], wr=["g2"])
        V(lambda e: e.tensor_tensor(out=r3g(gs), in0=r3g(g1), in1=r3g(g2), op=ALU.add),
          rd=["g1", "g2", "gs"], wr=["gs"])
        V(lambda e: e.tensor_reduce(out=thr[:, 0:NQ], in_=r3g(gs), axis=AX, op=ALU.max),
          rd=["gs", "thr"], wr=["thr"])
        V(lambda e: e.tensor_tensor(out=r3g(g1), in0=r3g(gs), in1=bq(thr, NGRP), op=ALU.is_ge),
          rd=["gs", "thr", "g1"], wr=["g1"])
        V(lambda e: e.scalar_tensor_tensor(out=r3g(g2), in0=r3g(g1), scalar=-1e30, in1=r3g(gs),
                                           op0=ALU.mult, op1=ALU.add), rd=["g1", "gs", "g2"], wr=["g2"])
        V(lambda e: e.tensor_reduce(out=thr[:, 0:NQ], in_=r3g(g2), axis=AX, op=ALU.max),
          rd=["g2", "thr"], wr=["thr"])
        V(lambda e: e.tensor_tensor(out=r3g(g1), in0=r3g(gs), in1=bq(thr, NGRP), op=ALU.is_ge),
          rd=["gs", "thr", "g1"], wr=["g1"])
        V(lambda e: e.tensor_tensor(out=r4(msk), in0=r4(choice), in1=bqg(g1), op=ALU.mult),
          rd=["choice", "g1", "msk"], wr=["msk"])
        for it in range(TOPK):
            V(lambda e: e.tensor_reduce(out=thr[:, 0:NQ], in_=r3(msk), axis=AX, op=ALU.max),
              rd=["msk", "thr"], wr=["thr"])
            if it < TOPK - 1:
                V(lambda e: e.tensor_tensor(out=r3(eqs), in0=r3(msk), in1=bq(thr, NEXP), op=ALU.is_ge),
                  rd=["msk", "thr", "eqs"], wr=["eqs"])
                V(lambda e: e.scalar_tensor_tensor(out=r3(msk), in0=r3(eqs), scalar=-1e30, in1=r3(msk),
                                                   op0=ALU.mult, op1=ALU.add), rd=["eqs", "msk"], wr=["msk"])
        V(lambda e: e.tensor_tensor(out=r4(eqs), in0=r4(choice), in1=bqg(g1), op=ALU.mult),
          rd=["choice", "g1", "eqs"], wr=["eqs"])
        V(lambda e: e.tensor_tensor(out=r3(eqs), in0=r3(eqs), in1=bq(thr, NEXP), op=ALU.is_ge),
          rd=["eqs", "thr"], wr=["eqs"])
        V(lambda e: e.tensor_tensor(out=r3(Wt), in0=r3(scores), in1=r3(eqs), op=ALU.mult),
          rd=["scores", "eqs", "Wt"], wr=["Wt"])
        V(lambda e: e.tensor_reduce(out=thr[:, 0:NQ], in_=r3(Wt), axis=AX, op=ALU.add),
          rd=["Wt", "thr"], wr=["thr"])
        V(lambda e: e.tensor_scalar(out=thr[:, 0:NQ], in0=thr[:, 0:NQ], scalar1=1e-20, scalar2=None,
                                    op0=ALU.add), rd=["thr"], wr=["thr"])
        V(lambda e: e.reciprocal(out=thr[:, 0:NQ], in_=thr[:, 0:NQ]), rd=["thr"], wr=["thr"])
        V(lambda e: e.scalar_tensor_tensor(out=r3(Wt), in0=r3(Wt), scalar=RSCALE, in1=bq(thr, NEXP),
                                           op0=ALU.mult, op1=ALU.mult), rd=["Wt", "thr"], wr=["Wt"])
        V(lambda e: e.tensor_copy(out=Wtb[:], in_=Wt[:]), rd=["Wt", "Wtb"], wr=["Wtb"])
        for ql in range(NQ):
            PE(lambda e, ql=ql: e.transpose(pbf[0:NEXP, 0:128], Wtb[:, ql * NEXP:(ql + 1) * NEXP],
                                            identb[:]), rd=["Wtb", "identb", "PBF"], wr=["PBF"])
            A_(lambda e, ql=ql: e.copy(out=WT[:, ql * 128:(ql + 1) * 128], in_=pbf[0:NEXP, 0:128]),
               rd=["PBF", "WT"], wr=["WT"])
        SY(lambda e: e.dma_start(out=wt_d[:, s * 1024:(s + 1) * 1024], in_=WT[:]),
           rd=["WT"], wr=[f"wtd{s}"])
        Dreg.close()

    def make_E(sfx):
        Ereg = ExitStack()

        def sbE(name, shape, dt):
            return Ereg.enter_context(nc.sbuf_tensor(name + sfx, shape, dt))

        t = {}
        t["mb"] = [[sbE(f"mb{i}{j}", [128, 4096], BF16) for j in range(2)] for i in range(2)]
        t["dscr"] = [sbE(f"dscr{i}", [128, CH], F32) for i in range(2)]
        t["tts"] = [sbE(f"tt{i}", [128, 8 * CH], BF16) for i in range(2)]
        t["Wb"] = [sbE(f"Wb{i}", [128, HT], BF16) for i in range(2)]
        t["sg"] = [sbE(f"sg_e{i}", [128, CH], BF16) for i in range(2)]
        t["ue"] = [sbE(f"ue_e{i}", [128, CH], BF16) for i in range(2)]
        t["ts"] = [sbE(f"ts_e{i}", [128, CH], BF16) for i in range(2)]
        t["xfF"] = [sbE(f"xfF{i}", [128, H], F32) for i in range(2)]
        return Ereg, t

    def load_mat(t, buf, key, eidx, which):
        src_key = {"eg": "egbf", "eu": "eubf", "ed": "edbf"}[which]
        if eidx < NEXP:
            dram = {"eg": egbf_d, "eu": eubf_d, "ed": edbf_d}[which]
            wid = H if which == "ed" else 512
            SY(lambda e, dram=dram, wid=wid, buf=buf: e.dma_start(
                out=buf[:].rearrange("p (k i) -> p k i", i=wid),
                in_=dram[eidx].rearrange("(k p) i -> p k i", p=128)),
               rd=[key, src_key], wr=[key])
        else:
            hb = (eidx - NEXP) * 512
            if which == "eg":
                SY(lambda e, buf=buf: e.dma_start(out=buf[:].rearrange("p (k i) -> p k i", i=512),
                                                  in_=sgbf_d[:, hb:hb + 512].rearrange("(k p) i -> p k i", p=128)),
                   rd=[key, "sgbf"], wr=[key])
            elif which == "eu":
                SY(lambda e, buf=buf: e.dma_start(out=buf[:].rearrange("p (k i) -> p k i", i=512),
                                                  in_=subf_d[:, hb:hb + 512].rearrange("(k p) i -> p k i", p=128)),
                   rd=[key, "subf"], wr=[key])
            else:
                SY(lambda e, buf=buf: e.dma_start(out=buf[:].rearrange("p (k i) -> p k i", i=H),
                                                  in_=sdbf_d[hb:hb + 512, :].rearrange("(k p) i -> p k i", p=128)),
                   rd=[key, "sdbf"], wr=[key])

    def load_Wb(t, s, si, eidx):
        if eidx < NEXP:
            SY(lambda e: e.dma_start(out=t["Wb"][si][:],
                                     in_=wt_d[eidx:eidx + 1, s * 1024:(s + 1) * 1024].partition_broadcast(128)),
               rd=[f"wtd{s}", f"Wbb{si}"], wr=[f"Wbb{si}"])

    def expert_gu(t, si, lc, routed, egb, egk, eub, euk):
        lo = (lc % 2) * CH
        for im in range(4):
            bank, bkey = pb[im % 2], f"PB{im % 2}"
            for kk in range(NKH):
                PE(lambda e, im=im, kk=kk, bank=bank, egb=egb: e.matmul(
                    bank[:], egb[:, kk * 512 + im * 128:kk * 512 + (im + 1) * 128],
                    xn2T[:, kk * HT + lo:kk * HT + lo + CH],
                    start=(kk == 0), stop=(kk == NKH - 1)),
                   rd=[egk, f"x2T{kk}"], wr=[bkey])
            bu_, buk = pb[2 + (im % 3)], f"PB{2 + (im % 3)}"
            for kk in range(NKH):
                PE(lambda e, im=im, kk=kk, bu_=bu_, eub=eub: e.matmul(
                    bu_[:], eub[:, kk * 512 + im * 128:kk * 512 + (im + 1) * 128],
                    xn2T[:, kk * HT + lo:kk * HT + lo + CH],
                    start=(kk == 0), stop=(kk == NKH - 1)),
                   rd=[euk, f"x2T{kk}"], wr=[buk])
            il = lc % 2
            tc = (il * 4 + im) * CH
            jj = im % 2
            sg, sgk = t["sg"][jj], f"sg_e{jj}"
            ue, uek = t["ue"][jj], f"ue_e{jj}"
            ts_, tsk = t["ts"][jj], f"ts_e{jj}"
            A_(lambda e, bank=bank, sg=sg: e.activation(out=sg[:], in_=bank[:], func=AF.Silu),
               rd=[bkey, sgk], wr=[sgk])
            A_(lambda e, bu_=bu_, ue=ue: e.copy(out=ue[:], in_=bu_[:]),
               rd=[buk, uek], wr=[uek])
            PL(lambda e, sg=sg, ue=ue, ts_=ts_: e.tensor_tensor(out=ts_[:], in0=sg[:], in1=ue[:],
                                                                op=ALU.mult),
               rd=[sgk, uek, tsk], wr=[tsk])
            if routed:
                PL(lambda e, tc=tc, lo=lo, si=si, ts_=ts_: e.tensor_tensor(
                    out=t["tts"][si][:, tc:tc + CH], in0=ts_[:],
                    in1=t["Wb"][si][:, lo:lo + CH], op=ALU.mult),
                   rd=[tsk, f"Wbb{si}", f"tt{si}_{il}_{im}"], wr=[f"tt{si}_{il}_{im}"])
            else:
                PL(lambda e, tc=tc, si=si, ts_=ts_: e.tensor_copy(out=t["tts"][si][:, tc:tc + CH],
                                                                  in_=ts_[:]),
                   rd=[tsk, f"tt{si}_{il}_{im}"], wr=[f"tt{si}_{il}_{im}"])

    def down_pair(t, lc, first_pair, ed0, ed0k, ed1, ed1k):
        eds = (ed0, ed1)
        edks = (ed0k, ed1k)
        for ls in range(4):
            q = lc * 4 + ls
            for hh in range(2):
                gi = (ls * 2 + hh) % 2
                bank, bkey = (pb[0], "PB0") if gi == 0 else (pb[1], "PB1")
                il = lc % 2
                mms = [(si, im) for si in (0, 1) for im in range(4)]
                for j, (si, im) in enumerate(mms):
                    tc = (il * 4 + im) * CH
                    PE(lambda e, si=si, im=im, ls=ls, hh=hh, bank=bank, tc=tc,
                       st=(j == 0), sp=(j == len(mms) - 1): e.matmul(
                        bank[:], t["tts"][si][:, tc + ls * 128:tc + (ls + 1) * 128],
                        eds[si][:, im * H + hh * 512:im * H + (hh + 1) * 512],
                        start=st, stop=sp),
                       rd=[f"tt{si}_{il}_{im}", edks[si]], wr=[bkey])
                ds = t["dscr"][(ls * 2 + hh) % 2]
                dk = f"dscr{(ls * 2 + hh) % 2}"
                A_(lambda e, bank=bank, ds=ds: e.copy(out=ds[:], in_=bank[:]),
                   rd=[bkey, dk], wr=[dk])
                aop = ALU.bypass if first_pair else ALU.add
                GD(lambda e, q=q, hh=hh, ds=ds, aop=aop: e.dma_start(
                    out=acc_d[q * 128:(q + 1) * 128, hh * 512:(hh + 1) * 512], in_=ds[:],
                    accum_op=aop),
                   rd=[dk, f"acc{q}_{hh}"], wr=[f"acc{q}_{hh}"])

    def emit_F(t, lc):
        ph = P.phase
        P.phase = "F"
        for q in range(lc * 4, lc * 4 + 4):
            j = q % 2
            xb, xk = t["xfF"][j], f"xfF{j}"
            SY(lambda e, q=q, xb=xb: e.dma_start(out=xb[:], in_=x1_d[q * 128:(q + 1) * 128, :]),
               rd=[f"x1d{q // 4}", xk], wr=[xk])
            GD(lambda e, q=q, xb=xb: e.dma_start(out=xb[:], in_=acc_d[q * 128:(q + 1) * 128, :],
                                                 accum_op=ALU.add),
               rd=[f"acc{q}_0", f"acc{q}_1", xk], wr=[xk])
            SY(lambda e, q=q, xb=xb: e.dma_start(out=out_d[q * 128:(q + 1) * 128, :], in_=xb[:]),
               rd=[xk], wr=[f"out{q}"])
        P.phase = ph

    def sweep_E(t, s, units=None):
        lcs = (2 * s, 2 * s + 1)
        ui = 0

        def step():
            nonlocal ui
            if units is not None and ui < len(units):
                units[ui]()
                ui += 1

        def bkey(i, j):
            return f"mb{i}{j}"

        load_mat(t, t["mb"][0][0], bkey(0, 0), 0, "eg")
        load_mat(t, t["mb"][0][1], bkey(0, 1), 0, "eu")
        load_mat(t, t["mb"][1][0], bkey(1, 0), 1, "eg")
        load_mat(t, t["mb"][1][1], bkey(1, 1), 1, "eu")
        load_Wb(t, s, 0, 0)
        load_Wb(t, s, 1, 1)
        for pair in range(17):
            c = pair % 2
            e0, e1 = 2 * pair, 2 * pair + 1
            eg0, eu0 = t["mb"][c][0], t["mb"][c][1]
            eg1, eu1 = t["mb"][1 - c][0], t["mb"][1 - c][1]
            k00, k01 = bkey(c, 0), bkey(c, 1)
            k10, k11 = bkey(1 - c, 0), bkey(1 - c, 1)
            for lc in lcs:
                P.phase = f"E{s}"
                expert_gu(t, 0, lc, e0 < NEXP, eg0, k00, eu0, k01)
                step()
                P.phase = f"E{s}"
                expert_gu(t, 1, lc, e1 < NEXP, eg1, k10, eu1, k11)
                step()
            P.phase = f"E{s}"
            load_mat(t, t["mb"][c][0], k00, e0, "ed")
            load_mat(t, t["mb"][c][1], k01, e1, "ed")
            if e0 + 2 < 34:
                load_mat(t, t["mb"][1 - c][0], k10, e0 + 2, "eg")
                load_mat(t, t["mb"][1 - c][1], k11, e0 + 2, "eu")
                load_Wb(t, s, 0, e0 + 2)
            for lc in lcs:
                P.phase = f"E{s}"
                down_pair(t, lc, pair == 0, t["mb"][c][0], k00, t["mb"][c][1], k01)
                if pair == 16:
                    emit_F(t, lc)
                step()
            if e1 + 2 < 34:
                P.phase = f"E{s}"
                load_mat(t, t["mb"][c][0], k00, e1 + 2, "eg")
                load_mat(t, t["mb"][c][1], k01, e1 + 2, "eu")
                load_Wb(t, s, 1, e1 + 2)
        while units is not None and ui < len(units):
            units[ui]()
            ui += 1

    P.phase = "hist0"
    emit_hist(0)
    for k in (0, 1):
        P.phase = f"A{k}"
        phase_A(k)
        P.phase = f"B{k}"
        for part in range(4):
            phase_B(k, range(part * 4, part * 4 + 4))
    P.phase = "CC0"
    emit_CC(0)
    Creg0, Ct0 = make_C("_c0")
    for k in (0, 1):
        P.phase = f"C{k}"
        phase_C(k, Ct0)
    Creg0.close()
    P.barrier()
    P.phase = "D0"
    emit_D(0, "_d0")
    P.barrier()

    def u_hist1():
        P.phase = "hist1"
        V(lambda e: e.memset(hcarry[:], 0.0), rd=HC + ["hcarry"], wr=["hcarry"] + HC)
        V(lambda e: e.memset(ccarry[:], 0.0), rd=CC + ["ccarry"], wr=["ccarry"] + CC)
        emit_hist(1)

    units = [u_hist1]
    units.append(lambda: (setattr(P, "phase", "A2"), phase_A(2)))
    for part in range(NST):
        for mh in range(2):
            units.append(lambda part=part, mh=mh: (setattr(P, "phase", "B2"),
                                                   phase_B(2, range(part, part + 1),
                                                           range(mh * 8, mh * 8 + 8))))
    units.append(lambda: (setattr(P, "phase", "A3"), phase_A(3)))
    for part in range(NST):
        for mh in range(2):
            units.append(lambda part=part, mh=mh: (setattr(P, "phase", "B3"),
                                                   phase_B(3, range(part, part + 1),
                                                           range(mh * 8, mh * 8 + 8))))
    units.append(lambda: (setattr(P, "phase", "CC1"), emit_CC(1)))

    Ereg0, Et0 = make_E("_e0")
    sweep_E(Et0, 0, units)
    Ereg0.close()
    P.barrier()

    Creg1, Ct1 = make_C("_c1")
    for k in (2, 3):
        P.phase = f"C{k}"
        phase_C(k, Ct1)
    Creg1.close()
    P.barrier()
    P.phase = "D1"
    emit_D(1, "_d1")
    P.barrier()
    Ereg1, Et1 = make_E("_e1")
    sweep_E(Et1, 1)
    Ereg1.close()

    P.emit("dve", lambda e: e.nop(), rd=[f"out{q}" for q in range(16)], wr=[])
    AB.close()

    sems = {}
    sem_es = ExitStack()
    sem_names = ["pe", "act", "dve", "pool"] + \
        [f"sync{i}" for i in range(Prog.NSLOT)] + [f"gdma{i}" for i in range(Prog.NSLOT)]
    for s in sem_names:
        sems[s] = sem_es.enter_context(nc.semaphore(f"sem_{s}"))
    run = P.materialize(sems)
    with nc.Block() as block:
        @block.tensor
        def _(eng):
            run(("pe",), eng, nc)

        @block.scalar
        def _(eng):
            run(("act",), eng, nc)

        @block.vector
        def _(eng):
            run(("dve",), eng, nc)

        @block.gpsimd
        def _(eng):
            run(("gdma", "pool"), eng, nc)

        @block.sync
        def _(eng):
            run(("sync",), eng, nc)
    ES.close()
    sem_es.close()
    nc.finalize()
    return nc


def prep_inputs(inputs):
    x = np.asarray(inputs["L0_attn_x"], np.float32)
    shared = {}
    shared["w_in_f"] = np.ascontiguousarray(inputs["L0_attn_w_in"], dtype=np.float32)
    shared["b_in_l"] = np.ascontiguousarray(np.asarray(inputs["L0_attn_b_in"], np.float32).reshape(32, 128).T)
    wc = np.asarray(inputs["L0_attn_w_conv"], np.float32)
    shared["wconv_l"] = np.ascontiguousarray(wc.reshape(TW, NDT, 128).transpose(2, 1, 0).reshape(128, NDT * TW))
    shared["bconv_l"] = np.ascontiguousarray(np.asarray(inputs["L0_attn_b_conv"], np.float32).reshape(NDT, 128).T)
    shared["wx_f"] = np.ascontiguousarray(inputs["L0_attn_w_x"], dtype=np.float32)
    shared["wdt_f"] = np.ascontiguousarray(inputs["L0_attn_w_dt"], dtype=np.float32)
    al = np.asarray(inputs["L0_attn_A_log"], np.float32)
    shared["alog_l"] = np.ascontiguousarray(al.reshape(NDT, 128, NST).transpose(1, 0, 2).reshape(128, NDT * NST))
    shared["d_l"] = np.ascontiguousarray(np.asarray(inputs["L0_attn_D"], np.float32).reshape(NDT, 128).T)
    shared["wout_f"] = np.ascontiguousarray(inputs["L0_attn_w_out"], dtype=np.float32)
    shared["gwT_f"] = np.ascontiguousarray(np.asarray(inputs["L0_ffn_gate_w"], np.float32).T)
    shared["gb_row"] = np.ascontiguousarray(np.asarray(inputs["L0_ffn_gate_bias"], np.float32).reshape(1, NEXP))
    shared["egT_f"] = np.ascontiguousarray(np.asarray(inputs["L0_ffn_expert_gate"], np.float32).transpose(0, 2, 1))
    shared["euT_f"] = np.ascontiguousarray(np.asarray(inputs["L0_ffn_expert_up"], np.float32).transpose(0, 2, 1))
    shared["edT_f"] = np.ascontiguousarray(np.asarray(inputs["L0_ffn_expert_down"], np.float32).transpose(0, 2, 1))
    shared["sgT_f"] = np.ascontiguousarray(np.asarray(inputs["L0_ffn_shared_gate"], np.float32).T)
    shared["suT_f"] = np.ascontiguousarray(np.asarray(inputs["L0_ffn_shared_up"], np.float32).T)
    shared["sdT_f"] = np.ascontiguousarray(np.asarray(inputs["L0_ffn_shared_down"], np.float32).T)
    shared["ident_bf"] = np.eye(128, dtype=ml_dtypes.bfloat16)
    shared["ident_f"] = np.eye(128, dtype=np.float32)

    in_maps = []
    HTOK = TOK // 2
    for c in range(NCORE):
        lo = c * HTOK
        parts = []
        for b in range(2):
            if c == 0:
                parts.append(np.concatenate([np.zeros((3, H), np.float32),
                                             x[b, lo:lo + HTOK]], axis=0))
            else:
                parts.append(x[b, lo - 3:lo + HTOK])
        m = dict(shared)
        m["x_shard"] = np.ascontiguousarray(np.concatenate(parts, axis=0))
        m["mask_in"] = np.array([[0.0 if c == 0 else 1.0]], np.float32)
        pm = np.zeros((8, 1), np.float32)
        pm[c, 0] = 1.0
        m["posmask"] = pm
        in_maps.append(m)
    return in_maps


def kernel(**inputs):
    in_maps = prep_inputs(inputs)
    nc = make_program(do_cc=True)
    res = run_bass_kernel_spmd(nc, in_maps, core_ids=list(range(NCORE)))
    out = np.empty((Bb, Lb, H), np.float32)
    HTOK = TOK // 2
    for c in range(NCORE):
        r = res.results[c]["out"]
        out[0, c * HTOK:(c + 1) * HTOK, :] = r[0:HTOK]
        out[1, c * HTOK:(c + 1) * HTOK, :] = r[HTOK:TOK]
    return out


# version 18
# speedup vs baseline: 1.2563x; 1.1549x over previous
import numpy as np
import ml_dtypes
from contextlib import ExitStack

import concourse.bass as bass
import concourse.mybir as mybir
from concourse.bass_utils import run_bass_kernel_spmd

F32 = mybir.dt.float32
BF16 = mybir.dt.bfloat16
AF = mybir.ActivationFunctionType
ALU = mybir.AluOpType
AX = mybir.AxisListType.X

Bb, Lb, H, DIN, NST, DTR, TW = 2, 8192, 1024, 2048, 16, 64, 4
NCORE = 8
TOK = 2048
CH = 512
NCHUNK = TOK // CH
NDT = DIN // 128
NKH = H // 128
NEXP, TOPK, NGRP = 32, 8, 4
RSCALE = 2.5


class Prog:
    ENGS = ("pe", "act", "dve", "pool", "sync", "gdma")
    NSLOT = 8

    def __init__(self):
        self.ops = []
        self.state = {}
        self.bar = set()
        self.dma_count = {"sync": 0, "gdma": 0}
        self.slot_last = {}
        self.phase = "setup"

    def emit(self, eng, fn, rd=(), wr=(), dma=False, inc=None, tag=""):
        idx = len(self.ops)
        op = dict(eng=eng, fn=fn, dma=dma, deps=set(), need_inc=False, inc=inc, tag=tag, phase=self.phase)
        if dma:
            slot = self.dma_count[eng] % self.NSLOT
            self.dma_count[eng] += 1
            op["sem_name"] = f"{eng}{slot}"
            op["need_inc"] = True
            prev = self.slot_last.get((eng, slot))
            if prev is not None:
                op["deps"].add(prev)
            self.slot_last[(eng, slot)] = idx
        else:
            op["sem_name"] = eng
        for b in self.bar:
            if self._xdep(b, eng, dma):
                op["deps"].add(b)
        for k in rd:
            st = self.state.get(k)
            if st is not None and st["w"] is not None and self._xdep(st["w"], eng, dma):
                op["deps"].add(st["w"])
        for k in wr:
            st = self.state.get(k)
            if st is not None:
                if st["w"] is not None and self._xdep(st["w"], eng, dma):
                    op["deps"].add(st["w"])
                for r in st["r"]:
                    if self._xdep(r, eng, dma):
                        op["deps"].add(r)
        self.ops.append(op)
        for k in rd:
            self.state.setdefault(k, {"w": None, "r": []})["r"].append(idx)
        for k in wr:
            self.state[k] = {"w": idx, "r": []}
        for d in op["deps"]:
            self.ops[d]["need_inc"] = True
        return idx

    def _xdep(self, prod_idx, eng, dma):
        p = self.ops[prod_idx]
        if p["dma"] or dma:
            return True
        if p["eng"] != eng:
            return True
        return eng != "pe"

    def barrier(self):
        last = {}
        for i, op in enumerate(self.ops):
            last[op["eng"]] = i
        self.bar = set(last.values())
        for i in self.bar:
            self.ops[i]["need_inc"] = True

    def materialize(self, sems):
        val = {s: 0 for s in sems}
        for op in self.ops:
            if op["need_inc"] or op["inc"] is not None:
                s = op["sem_name"]
                step = op["inc"] if op["inc"] is not None else (16 if op["dma"] else 1)
                val[s] += step
                op["sem_target"] = (s, val[s])
                op["step"] = step
            else:
                op["sem_target"] = None
        waited = {e: {} for e in self.ENGS}
        for op in self.ops:
            w = {}
            for d in op["deps"]:
                s, v = self.ops[d]["sem_target"]
                w[s] = max(w.get(s, 0), v)
            ww = []
            for s, v in w.items():
                if waited[op["eng"]].get(s, 0) < v:
                    waited[op["eng"]][s] = v
                    ww.append((s, v))
            op["waits"] = ww

        def run(eng_names, eng, nc=None):
            for op in self.ops:
                if op["eng"] not in eng_names:
                    continue
                if nc is not None:
                    with nc.named_scope(op["phase"]):
                        for s, v in op["waits"]:
                            eng.wait_ge(sems[s], v)
                        inst = op["fn"](eng)
                        if op["sem_target"] is not None:
                            inst.then_inc(sems[op["sem_name"]], op["step"])
                else:
                    for s, v in op["waits"]:
                        eng.wait_ge(sems[s], v)
                    inst = op["fn"](eng)
                    if op["sem_target"] is not None:
                        inst.then_inc(sems[op["sem_name"]], op["step"])
        return run


def make_program(do_cc=True):
    nc = bass.Bass("TRN2", target_bir_lowering=False)

    def din(name, shape, dt=F32):
        return nc.dram_tensor(name, shape, dt, kind="ExternalInput").ap()

    x_d = din("x_shard", [TOK + 3, H])
    mask_d = din("mask_in", [1, 1])
    posm_d = din("posmask", [4, 1])
    win_d = din("w_in_f", [H, 2 * DIN])
    binl_d = din("b_in_l", [128, 32])
    wconv_d = din("wconv_l", [128, NDT * TW])
    bconv_d = din("bconv_l", [128, NDT])
    wx_d = din("wx_f", [DIN, DTR + 2 * NST])
    wdt_d = din("wdt_f", [DTR, DIN])
    alog_d = din("alog_l", [128, NDT * NST])
    dl_d = din("d_l", [128, NDT])
    wout_d = din("wout_f", [DIN, H])
    gwT_d = din("gwT_f", [H, NEXP])
    gb_d = din("gb_row", [1, NEXP])
    egT_d = din("egT_f", [NEXP, H, 512])
    euT_d = din("euT_f", [NEXP, H, 512])
    edT_d = din("edT_f", [NEXP, 512, H])
    sgT_d = din("sgT_f", [H, H])
    suT_d = din("suT_f", [H, H])
    sdT_d = din("sdT_f", [H, H])
    identb_d = din("ident_bf", [128, 128], BF16)
    identf_d = din("ident_f", [128, 128], F32)

    out_d = nc.dram_tensor("out", [TOK, H], F32, kind="ExternalOutput").ap()

    def scr(name, shape, dt):
        return nc.dram_tensor(name, shape, dt, kind="Internal").ap()

    winbf_d = scr("winbf", [H, 2 * DIN], BF16)
    woutbf_d = scr("woutbf", [DIN, H], BF16)
    bc_d = scr("bc_scr", [NCHUNK, 32, CH], BF16)
    ub_d = scr("ub_scr", [NCHUNK, 128, NDT * CH], BF16)
    sres_d = scr("sres_scr", [NCHUNK, 128, NDT * CH], BF16)
    y_d = scr("y_scr", [NCHUNK, 128, NDT * CH], BF16)
    cums_d = scr("cums_scr", [NCHUNK, 128, NDT * CH], BF16)
    x1_d = scr("x1_scr", [TOK, H], F32)
    wt_d = scr("wt_scr", [NEXP, TOK], BF16)
    cc_in = scr("cc_in", [128, 272], F32)
    cc_out = scr("cc_out", [4 * 128, 272], F32)

    P = Prog()
    ES = ExitStack()

    def sbG(name, shape, dt):
        return ES.enter_context(nc.sbuf_tensor(name, shape, dt))

    def psG(name, shape, dt):
        return ES.enter_context(nc.psum_tensor(name, shape, dt))

    binl = sbG("binl", [128, 32], F32)
    wconv = sbG("wconv", [128, NDT * TW], F32)
    bconv = sbG("bconv", [128, NDT], F32)
    wxbf = sbG("wxbf", [128, NDT * (DTR + 2 * NST)], BF16)
    wdtbf = sbG("wdtbf", [64, DIN], BF16)
    negA = sbG("negA", [128, NDT * NST], F32)
    dl = sbG("dl", [128, NDT], F32)
    gwTbf = sbG("gwTbf", [128, NKH * NEXP], BF16)
    gbb = sbG("gbb", [128, NEXP], F32)
    identb = sbG("identb", [128, 128], BF16)
    identf = sbG("identf", [128, 128], F32)
    hcarry = sbG("hcarry", [128, NDT * NST], F32)
    ccarry = sbG("ccarry", [128, NDT], F32)
    h_in = sbG("h_in", [128, NDT * NST], F32)
    maskb = sbG("maskb", [128, 1], F32)
    pmb = sbG("pmb", [128, 4], F32)
    consts = sbG("consts", [128, 8], F32)

    stat = sbG("stat", [128, 64], F32)
    tsA = [sbG(f"tsA{i}", [128, CH], F32) for i in range(2)]
    esA = [sbG(f"esA{i}", [128, CH], F32) for i in range(2)]
    cbuf = [sbG(f"cbuf{i}", [128, CH], BF16) for i in range(2)]

    pb = [psG(f"pb{i}", [128, 512], F32) for i in range(7)]
    pbf = psG("pbf", [128, 1024], BF16)

    def V(fn, rd=(), wr=(), tag=""):
        P.emit("dve", fn, rd, wr, tag=tag)

    def A_(fn, rd=(), wr=(), tag=""):
        P.emit("act", fn, rd, wr, tag=tag)

    def PE(fn, rd=(), wr=(), tag=""):
        P.emit("pe", fn, rd, wr, tag=tag)

    def SY(fn, rd=(), wr=(), tag=""):
        P.emit("sync", fn, rd, wr, dma=True, tag=tag)

    def GD(fn, rd=(), wr=(), tag=""):
        P.emit("gdma", fn, rd, wr, dma=True, tag=tag)

    def PL(fn, rd=(), wr=(), inc=None, tag=""):
        P.emit("pool", fn, rd, wr, inc=inc, tag=tag)

    HC = [f"hcar{c}" for c in range(NDT * NST)]
    CC = [f"ccar{m}" for m in range(NDT)]

    V(lambda e: e.memset(consts[:, 0:1], 1e-5), wr=["consts"])
    V(lambda e: e.memset(consts[:, 1:2], -0.5), wr=["consts"])
    V(lambda e: e.memset(consts[:, 2:3], 1.0), wr=["consts"])
    V(lambda e: e.memset(hcarry[:], 0.0), wr=["hcarry"] + HC)
    V(lambda e: e.memset(ccarry[:], 0.0), wr=["ccarry"] + CC)
    V(lambda e: e.memset(h_in[:], 0.0), wr=["h_in"])

    SY(lambda e: e.dma_start(out=binl[:], in_=binl_d[:]), wr=["binl"])
    SY(lambda e: e.dma_start(out=wconv[:], in_=wconv_d[:]), wr=["wconv"])
    SY(lambda e: e.dma_start(out=bconv[:], in_=bconv_d[:]), wr=["bconv"])
    SY(lambda e: e.dma_start(out=dl[:], in_=dl_d[:]), wr=["dl"])
    SY(lambda e: e.dma_start(out=gbb[:], in_=gb_d.partition_broadcast(128)), wr=["gbb"])
    SY(lambda e: e.dma_start(out=identb[:], in_=identb_d[:]), wr=["identb"])
    SY(lambda e: e.dma_start(out=identf[:], in_=identf_d[:]), wr=["identf"])
    GD(lambda e: e.dma_start(out=maskb[:], in_=mask_d.partition_broadcast(128)), wr=["maskb"])
    GD(lambda e: e.dma_start(out=pmb[:], in_=posm_d.rearrange("k o -> o k").partition_broadcast(128)), wr=["pmb"])
    GD(lambda e: e.dma_start(out=wxbf[:].rearrange("p (k m) -> p k m", m=DTR + 2 * NST), in_=wx_d.rearrange("(k p) m -> p k m", p=128)), wr=["wxbf"])
    GD(lambda e: e.dma_start(out=wdtbf[:], in_=wdt_d[:]), wr=["wdtbf"])
    GD(lambda e: e.dma_start(out=gwTbf[:].rearrange("p (k m) -> p k m", m=NEXP), in_=gwT_d.rearrange("(k p) m -> p k m", p=128)), wr=["gwTbf"])
    GD(lambda e: e.dma_start(out=winbf_d[:], in_=win_d[:]), wr=["winbf"])
    GD(lambda e: e.dma_start(out=woutbf_d[:], in_=wout_d[:]), wr=["woutbf"])

    AB = ExitStack()

    def sbA(name, shape, dt):
        return AB.enter_context(nc.sbuf_tensor(name, shape, dt))

    alog_sb = sbA("alog_sb", [128, NDT * NST], F32)
    t_scr = sbA("t_scr", [128, CH], F32)
    xcT = sbA("xcT", [128, NDT * 520], BF16)
    xf = sbA("xf", [128, 4 * H], F32)
    sq_scr = sbA("sq_scr", [128, H], F32)
    xn = sbA("xn", [128, 4 * H], BF16)
    xnT = sbA("xnT", [128, NKH * CH], BF16)
    wbufs = [sbA(f"wbuf{i}", [128, NKH * 128], BF16) for i in range(2)]
    sresT = sbA("sresT", [128, NDT * CH], BF16)
    ubfT = sbA("ubfT", [128, NDT * CH], BF16)
    deltaT = sbA("deltaT", [128, NDT * CH], BF16)
    dubT = sbA("dubT", [128, NDT * CH], BF16)
    dtbfT = sbA("dtbfT", [64, CH], BF16)
    bcs = sbA("bcs", [32, CH], F32)
    y_sb = sbA("y_sb", [128, NDT * CH], BF16)
    cums_sb = sbA("cums_sb", [128, CH], BF16)
    a2 = [sbA(f"a2_{i}", [128, CH], F32) for i in range(2)]
    bu2 = [sbA(f"bu2_{i}", [128, 2 * CH], BF16) for i in range(2)]
    hs2 = [sbA(f"hs2_{i}", [128, 2 * CH], BF16) for i in range(2)]
    bbuf = [sbA(f"bbuf{i}", [128, CH], BF16) for i in range(2)]
    xh_s = sbA("xh_s", [3, H], F32)
    xhn = sbA("xhn", [3, H], BF16)
    xhT = sbA("xhT", [128, NKH * 3], BF16)

    SY(lambda e: e.dma_start(out=alog_sb[:], in_=alog_d[:]), wr=["alog"])
    A_(lambda e: e.activation(out=negA[:], in_=alog_sb[:], func=AF.Exp), rd=["alog"], wr=["negA"])
    V(lambda e: e.tensor_scalar(out=negA[:], in0=negA[:], scalar1=-1.0, scalar2=None, op0=ALU.mult),
      rd=["negA"], wr=["negA"])

    def emit_ln(src_ap, dst_ap, nrow, src_key, dst_key, sq_buf, c0=0):
        A_(lambda e: e.activation(out=sq_buf[0:nrow, :], in_=src_ap, func=AF.Square,
                                  accum_out=stat[0:nrow, c0 + 0:c0 + 1]), rd=[src_key], wr=["sq", f"st0_{c0}"])
        V(lambda e: e.tensor_reduce(out=stat[0:nrow, c0 + 1:c0 + 2], in_=src_ap, axis=AX, op=ALU.add),
          rd=[src_key], wr=[f"st1_{c0}"])
        V(lambda e: e.tensor_scalar(out=stat[0:nrow, c0 + 2:c0 + 3], in0=stat[0:nrow, c0 + 1:c0 + 2], scalar1=1.0 / H,
                                    scalar2=None, op0=ALU.mult), rd=[f"st1_{c0}"], wr=[f"st2_{c0}"])
        V(lambda e: e.tensor_tensor(out=stat[0:nrow, c0 + 3:c0 + 4], in0=stat[0:nrow, c0 + 2:c0 + 3],
                                    in1=stat[0:nrow, c0 + 2:c0 + 3], op=ALU.mult), rd=[f"st2_{c0}"], wr=[f"st3_{c0}"])
        V(lambda e: e.scalar_tensor_tensor(out=stat[0:nrow, c0 + 4:c0 + 5], in0=stat[0:nrow, c0 + 0:c0 + 1],
                                           scalar=1.0 / H, in1=stat[0:nrow, c0 + 3:c0 + 4],
                                           op0=ALU.mult, op1=ALU.subtract),
          rd=[f"st0_{c0}", f"st3_{c0}"], wr=[f"st4_{c0}"])
        A_(lambda e: e.activation(out=stat[0:nrow, c0 + 5:c0 + 6], in_=stat[0:nrow, c0 + 4:c0 + 5], func=AF.Ln,
                                  bias=consts[0:nrow, 0:1]), rd=[f"st4_{c0}", "consts"], wr=[f"st5_{c0}"])
        A_(lambda e: e.activation(out=stat[0:nrow, c0 + 6:c0 + 7], in_=stat[0:nrow, c0 + 5:c0 + 6], func=AF.Exp,
                                  scale=consts[0:nrow, 1:2]), rd=[f"st5_{c0}", "consts"], wr=[f"st6_{c0}"])
        V(lambda e: e.tensor_tensor(out=stat[0:nrow, c0 + 7:c0 + 8], in0=stat[0:nrow, c0 + 2:c0 + 3],
                                    in1=stat[0:nrow, c0 + 6:c0 + 7], op=ALU.mult), rd=[f"st2_{c0}", f"st6_{c0}"], wr=[f"st7_{c0}"])
        V(lambda e: e.tensor_scalar(out=dst_ap, in0=src_ap, scalar1=stat[0:nrow, c0 + 6:c0 + 7],
                                    scalar2=stat[0:nrow, c0 + 7:c0 + 8], op0=ALU.mult, op1=ALU.subtract),
          rd=[src_key, f"st6_{c0}", f"st7_{c0}"], wr=[dst_key])

    SY(lambda e: e.dma_start(out=xh_s[:], in_=x_d[0:3, :]), wr=["xh"])
    emit_ln(xh_s[:], xhn[:], 3, "xh", "xhn", sq_scr, c0=32)
    for kk in range(NKH):
        PE(lambda e, kk=kk: e.transpose(pbf[:, kk * 128:kk * 128 + 3],
                                        xhn[0:3, kk * 128:(kk + 1) * 128], identb[0:3, 0:3]),
           rd=["xhn", "identb"], wr=["PBF"])
        A_(lambda e, kk=kk: e.copy(out=xhT[:, kk * 3:(kk + 1) * 3], in_=pbf[:, kk * 128:kk * 128 + 3]),
           rd=["PBF"], wr=["xhT"])
    for m in range(NDT):
        wb = wbufs[m % 2]
        SY(lambda e, m=m, wb=wb: e.dma_start(
            out=wb[:], in_=winbf_d.rearrange("(k p) f -> p k f", p=128)[:, :, m * 128:(m + 1) * 128]),
           rd=["winbf"], wr=[f"wb{m % 2}"])
        for kk in range(NKH):
            PE(lambda e, m=m, kk=kk, wb=wb: e.matmul(
                pb[0][0:128, 0:3], wb[:, kk * 128:(kk + 1) * 128], xhT[:, kk * 3:(kk + 1) * 3],
                start=(kk == 0), stop=(kk == NKH - 1)),
               rd=[f"wb{m % 2}", "xhT"], wr=["PB0"])
        A_(lambda e, m=m: e.activation(out=t_scr[:, 0:3], in_=pb[0][0:128, 0:3], func=AF.Identity,
                                       bias=binl[:, m:m + 1]), rd=["PB0", "binl"], wr=["t_scr"])
        V(lambda e, m=m: e.tensor_scalar(out=xcT[:, m * 520:m * 520 + 3], in0=t_scr[:, 0:3],
                                         scalar1=maskb[:], scalar2=None, op0=ALU.mult),
          rd=["t_scr", "maskb"], wr=[f"xch{m}"])

    def phase_A(k):
        for q in range(4):
            SY(lambda e, q=q: e.dma_start(out=xf[:, q * H:(q + 1) * H],
                                          in_=x_d[3 + k * CH + q * 128:3 + k * CH + (q + 1) * 128, :]),
               rd=["xf"], wr=["xf"], tag=f"ldx{k}")
        for p in range(4):
            emit_ln(xf[:, p * H:(p + 1) * H], xn[:, p * H:(p + 1) * H], 128, "xf", f"xn{p}", sq_scr, c0=p * 8)
            for kk in range(NKH):
                PE(lambda e, p=p, kk=kk: e.transpose(
                    pbf[:, p * 128:(p + 1) * 128],
                    xn[:, p * H + kk * 128:p * H + (kk + 1) * 128], identb[:]),
                   rd=[f"xn{p}", "identb"], wr=["PBF"])
                A_(lambda e, p=p, kk=kk: e.copy(
                    out=xnT[:, kk * CH + p * 128:kk * CH + (p + 1) * 128],
                    in_=pbf[:, p * 128:(p + 1) * 128]),
                   rd=["PBF"], wr=[f"xnT{kk}"])
        for m in range(2 * NDT):
            wb = wbufs[m % 2]
            SY(lambda e, m=m, wb=wb: e.dma_start(
                out=wb[:], in_=winbf_d.rearrange("(k p) f -> p k f", p=128)[:, :, m * 128:(m + 1) * 128]),
               rd=["winbf"], wr=[f"wb{m % 2}"])
            bank, bkey = pb[m % 2], f"PB{m % 2}"
            for kk in range(NKH):
                PE(lambda e, m=m, kk=kk, wb=wb, bank=bank: e.matmul(
                    bank[:], wb[:, kk * 128:(kk + 1) * 128], xnT[:, kk * CH:(kk + 1) * CH],
                    start=(kk == 0), stop=(kk == NKH - 1)),
                   rd=[f"wb{m % 2}", f"xnT{kk}"], wr=[bkey])
            if m < NDT:
                A_(lambda e, m=m, bank=bank: e.activation(
                    out=xcT[:, m * 520 + 3:m * 520 + 3 + CH], in_=bank[:], func=AF.Identity,
                    bias=binl[:, m:m + 1]), rd=[bkey, "binl"], wr=[f"xcd{m}"])
            else:
                mm = m - NDT
                A_(lambda e, mm=mm, m=m, bank=bank: e.activation(
                    out=sresT[:, mm * CH:(mm + 1) * CH], in_=bank[:], func=AF.Silu,
                    bias=binl[:, m:m + 1]), rd=[bkey, "binl"], wr=[f"sres{mm}"])
        SY(lambda e: e.dma_start(out=sres_d[k], in_=sresT[:]),
           rd=[f"sres{m}" for m in range(NDT)], wr=[f"sresd{k}"])
        for m in range(NDT):
            base = m * 520
            j = m % 2
            cs = tsA[j]
            ck = f"tsA{j}"
            V(lambda e, m=m, base=base, cs=cs: e.tensor_scalar(
                out=cs[:], in0=xcT[:, base:base + CH], scalar1=wconv[:, m * TW:m * TW + 1],
                scalar2=None, op0=ALU.mult), rd=[f"xcd{m}", f"xch{m}", "wconv", ck], wr=[ck])
            for kk in range(1, TW):
                V(lambda e, m=m, base=base, kk=kk, cs=cs: e.scalar_tensor_tensor(
                    out=cs[:], in0=xcT[:, base + kk:base + kk + CH],
                    scalar=wconv[:, m * TW + kk:m * TW + kk + 1], in1=cs[:],
                    op0=ALU.mult, op1=ALU.add),
                  rd=[f"xcd{m}", f"xch{m}", "wconv", ck], wr=[ck])
            A_(lambda e, m=m, cs=cs: e.activation(out=ubfT[:, m * CH:(m + 1) * CH], in_=cs[:],
                                           func=AF.Silu, bias=bconv[:, m:m + 1]),
               rd=[ck, "bconv"], wr=[f"ubf{m}"])
            V(lambda e, m=m, base=base: e.tensor_copy(out=xcT[:, base:base + 3],
                                                      in_=xcT[:, base + CH:base + CH + 3]),
              rd=[f"xcd{m}"], wr=[f"xch{m}"])
        SY(lambda e: e.dma_start(out=ub_d[k], in_=ubfT[:]),
           rd=[f"ubf{m}" for m in range(NDT)], wr=[f"ubd{k}"])
        for kk in range(NDT):
            PE(lambda e, kk=kk: e.matmul(pb[4][0:96, 0:CH], wxbf[:, kk * 96:(kk + 1) * 96],
                                         ubfT[:, kk * CH:(kk + 1) * CH],
                                         start=(kk == 0), stop=(kk == NDT - 1)),
               rd=[f"ubf{kk}", "wxbf"], wr=["PB4"])
        A_(lambda e: e.copy(out=dtbfT[:], in_=pb[4][0:64, 0:CH]), rd=["PB4"], wr=["dtbf"])
        A_(lambda e: e.copy(out=bcs[:], in_=pb[4][64:96, 0:CH]), rd=["PB4"], wr=["bcs"])
        GD(lambda e: e.dma_start(out=bc_d[k], in_=bcs[:]), rd=["bcs"], wr=[f"bcd{k}"])
        for m in range(NDT):
            bank, bkey = pb[5 + (m % 2)], f"PB{5 + (m % 2)}"
            PE(lambda e, m=m, bank=bank: e.matmul(bank[:], wdtbf[:, m * 128:(m + 1) * 128],
                                                  dtbfT[:], start=True, stop=True),
               rd=["dtbf", "wdtbf"], wr=[bkey])
            ej = esA[m % 2]
            ejk = f"esA{m % 2}"
            A_(lambda e, bank=bank, ej=ej: e.activation(out=ej[:], in_=bank[:], func=AF.Exp),
               rd=[bkey, ejk], wr=[ejk])
            A_(lambda e, m=m, ej=ej: e.activation(out=deltaT[:, m * CH:(m + 1) * CH], in_=ej[:],
                                           func=AF.Ln, bias=consts[:, 2:3]),
               rd=[ejk, "consts"], wr=[f"del{m}"])
            V(lambda e, m=m: e.tensor_tensor(out=dubT[:, m * CH:(m + 1) * CH],
                                             in0=deltaT[:, m * CH:(m + 1) * CH],
                                             in1=ubfT[:, m * CH:(m + 1) * CH], op=ALU.mult),
              rd=[f"del{m}", f"ubf{m}"], wr=[f"dub{m}"])

    def phase_B(k):
        for m in range(NDT):
            V(lambda e, m=m: e.tensor_tensor_scan(out=cums_sb[:], data0=consts[:, 2:3].broadcast_to((128, CH)),
                                                  data1=deltaT[:, m * CH:(m + 1) * CH],
                                                  initial=ccarry[:, m:m + 1],
                                                  op0=ALU.mult, op1=ALU.add),
              rd=[f"del{m}", "consts", f"ccar{m}"], wr=["cums"])
            V(lambda e, m=m: e.tensor_copy(out=ccarry[:, m:m + 1], in_=cums_sb[:, CH - 1:CH]),
              rd=["cums"], wr=[f"ccar{m}"])
            SY(lambda e, m=m: e.dma_start(out=cums_d[k, :, m * CH:(m + 1) * CH], in_=cums_sb[:]),
               rd=["cums"], wr=[f"cumsd{k}"])
        def load_bb(n):
            bB, bC = bbuf[n % 2], cbuf[n % 2]
            SY(lambda e, n=n, bB=bB: e.dma_start(
                out=bB[:], in_=bc_d[k, n:n + 1, :].partition_broadcast(128)),
               rd=[f"bcd{k}", f"bb{n % 2}"], wr=[f"bb{n % 2}"])
            SY(lambda e, n=n, bC=bC: e.dma_start(
                out=bC[:], in_=bc_d[k, NST + n:NST + n + 1, :].partition_broadcast(128)),
               rd=[f"bcd{k}", f"cb{n % 2}"], wr=[f"cb{n % 2}"])
        load_bb(0)
        for n in range(NST):
            bB, bC = bbuf[n % 2], cbuf[n % 2]
            if n + 1 < NST:
                load_bb(n + 1)
            for m in range(0, NDT, 2):
                col = n * NDT + m
                j = (m // 2) % 2
                bj, hj = bu2[j], hs2[j]
                bk2, hk = f"bu2_{j}", f"hs2_{j}"
                for dm in range(2):
                    aj, ak = a2[dm], f"a2_{dm}"
                    A_(lambda e, m=m, dm=dm, col=col, aj=aj: e.activation(
                        out=aj[:], in_=deltaT[:, (m + dm) * CH:(m + dm + 1) * CH], func=AF.Exp,
                        scale=negA[:, col + dm:col + dm + 1]),
                       rd=[f"del{m + dm}", "negA", ak], wr=[ak])
                V(lambda e, m=m, bB=bB, bj=bj: e.tensor_tensor(
                    out=bj[:].rearrange("p (d l) -> p d l", l=CH),
                    in0=dubT[:, m * CH:(m + 2) * CH].rearrange("p (d l) -> p d l", l=CH),
                    in1=bB[:].unsqueeze(1).broadcast_to((128, 2, CH)), op=ALU.mult),
                  rd=[f"dub{m}", f"dub{m + 1}", f"bb{n % 2}", bk2], wr=[bk2])
                for dm in range(2):
                    aj, ak = a2[dm], f"a2_{dm}"
                    V(lambda e, dm=dm, col=col, aj=aj, bj=bj, hj=hj: e.tensor_tensor_scan(
                        out=hj[:, dm * CH:(dm + 1) * CH], data0=aj[:],
                        data1=bj[:, dm * CH:(dm + 1) * CH],
                        initial=hcarry[:, col + dm:col + dm + 1], op0=ALU.mult, op1=ALU.add),
                      rd=[ak, bk2, f"hcar{col + dm}", hk], wr=[hk])
                V(lambda e, col=col, hj=hj: e.tensor_copy(
                    out=hcarry[:, col:col + 2],
                    in_=hj[:].rearrange("p (d l) -> p d l", l=CH)[:, :, CH - 1:CH].rearrange("p d o -> p (d o)")),
                  rd=[hk, f"hcar{col}", f"hcar{col + 1}"], wr=[f"hcar{col}", f"hcar{col + 1}"])
                V(lambda e, bC=bC, hj=hj: e.tensor_tensor(
                    out=hj[:].rearrange("p (d l) -> p d l", l=CH),
                    in0=hj[:].rearrange("p (d l) -> p d l", l=CH),
                    in1=bC[:].unsqueeze(1).broadcast_to((128, 2, CH)), op=ALU.mult),
                  rd=[hk, f"cb{n % 2}"], wr=[hk])
                if n == 0:
                    V(lambda e, m=m, hj=hj: e.tensor_copy(out=y_sb[:, m * CH:(m + 2) * CH], in_=hj[:]),
                      rd=[hk, f"y{m}", f"y{m + 1}"], wr=[f"y{m}", f"y{m + 1}"])
                else:
                    V(lambda e, m=m, hj=hj: e.tensor_tensor(out=y_sb[:, m * CH:(m + 2) * CH],
                                                            in0=y_sb[:, m * CH:(m + 2) * CH],
                                                            in1=hj[:], op=ALU.add),
                      rd=[hk, f"y{m}", f"y{m + 1}"], wr=[f"y{m}", f"y{m + 1}"])
        SY(lambda e: e.dma_start(out=y_d[k], in_=y_sb[:]),
           rd=[f"y{m}" for m in range(NDT)], wr=[f"yd{k}"])

    for k in range(NCHUNK):
        P.phase = f"A{k}"
        phase_A(k)
        P.phase = f"B{k}"
        phase_B(k)
    AB.close()
    P.phase = "CC"
    P.barrier()

    SY(lambda e: e.dma_start(out=cc_in[:, 0:256], in_=hcarry[:]), rd=HC, wr=["ccin"])
    SY(lambda e: e.dma_start(out=cc_in[:, 256:272], in_=ccarry[:]), rd=CC, wr=["ccin"])
    if do_cc:
        PL(lambda e: e.collective_compute(
            "AllGather", ALU.bypass, replica_groups=[[0, 1, 2, 3], [4, 5, 6, 7]],
            ins=[cc_in[:]], outs=[cc_out[:]]), rd=["ccin"], wr=["ccout"], inc=1)
    else:
        for r in range(4):
            GD(lambda e, r=r: e.dma_start(out=cc_out[r * 128:(r + 1) * 128, :], in_=cc_in[:]),
               rd=["ccin"], wr=["ccout"])
    agr = sbG("agr", [128, 4 * 272], F32)
    Hc = sbG("Hc", [128, 256], F32)
    Pt = sbG("Pt", [128, 256], F32)
    for r in range(3):
        SY(lambda e, r=r: e.dma_start(out=agr[:, r * 272:(r + 1) * 272],
                                      in_=cc_out[r * 128:(r + 1) * 128, :]),
           rd=["ccout"], wr=[f"agr{r}"])
    V(lambda e: e.tensor_copy(out=Hc[:], in_=agr[:, 0:256]), rd=["agr0"], wr=["Hc"])
    V(lambda e: e.tensor_scalar(out=h_in[:], in0=Hc[:], scalar1=pmb[:, 1:2], scalar2=None,
                                op0=ALU.mult), rd=["Hc", "pmb", "h_in"], wr=["h_in"])
    for r in (1, 2):
        V(lambda e, r=r: e.tensor_tensor(
            out=Pt[:].rearrange("p (n d) -> p n d", d=NDT),
            in0=negA[:].rearrange("p (n d) -> p n d", d=NDT),
            in1=agr[:, r * 272 + 256:r * 272 + 272].unsqueeze(1).broadcast_to((128, NST, NDT)),
            op=ALU.mult), rd=[f"agr{r}", "negA"], wr=["Pt"])
        A_(lambda e: e.activation(out=Pt[:], in_=Pt[:], func=AF.Exp), rd=["Pt"], wr=["Pt"])
        V(lambda e: e.tensor_tensor(out=Hc[:], in0=Pt[:], in1=Hc[:], op=ALU.mult),
          rd=["Pt", "Hc"], wr=["Hc"])
        V(lambda e, r=r: e.tensor_tensor(out=Hc[:], in0=Hc[:], in1=agr[:, r * 272:r * 272 + 256],
                                         op=ALU.add), rd=["Hc", f"agr{r}"], wr=["Hc"])
        V(lambda e, r=r: e.scalar_tensor_tensor(out=h_in[:], in0=Hc[:], scalar=pmb[:, r + 1:r + 2],
                                                in1=h_in[:], op0=ALU.mult, op1=ALU.add),
          rd=["Hc", "pmb", "h_in"], wr=["h_in"])

    lnh = sbG("lnh", [128, NDT * NST], F32)
    sgnh = sbG("sgnh", [128, NDT * NST], F32)
    e2bf = [sbG(f"e2bf{i}", [128, CH], BF16) for i in range(2)]
    V(lambda e: e.memset(consts[:, 4:5], 1e-38), rd=["consts"], wr=["consts"])
    A_(lambda e: e.activation(out=lnh[:], in_=h_in[:], func=AF.Abs), rd=["h_in"], wr=["lnh"])
    A_(lambda e: e.activation(out=sgnh[:], in_=h_in[:], func=AF.Sign), rd=["h_in"], wr=["sgnh"])
    A_(lambda e: e.activation(out=lnh[:], in_=lnh[:], func=AF.Ln, bias=consts[:, 4:5]),
       rd=["lnh", "consts"], wr=["lnh"])

    sq_scr2 = sbG("sq_scr2", [128, H], F32)
    xn2T = sbG("xn2T", [128, NKH * TOK], BF16)
    Creg = ExitStack()

    def sbC(name, shape, dt):
        return Creg.enter_context(nc.sbuf_tensor(name, shape, dt))

    cumub = sbC("cumub", [128, NDT * CH], BF16)
    sresC = sbC("sresC", [128, NDT * CH], BF16)
    yC = sbC("yC", [128, NDT * CH], F32)
    woutbf = sbC("woutbf_s", [128, NDT * H], BF16)
    ygT = cumub
    xfC = sbC("xfC", [128, 4 * H], F32)
    xn2 = sbC("xn2", [128, 4 * H], BF16)

    GD(lambda e: e.dma_start(out=woutbf[:].rearrange("p (k f) -> p k f", f=H), in_=woutbf_d.rearrange("(k p) f -> p k f", p=128)),
       rd=["woutbf"], wr=["woutsb"])

    def phase_C(k):
        SY(lambda e: e.dma_start(out=cumub[:], in_=cums_d[k]), rd=[f"cumsd{k}", "cumub"], wr=["cumub"])
        SY(lambda e: e.dma_start(out=sresC[:], in_=sres_d[k]), rd=[f"sresd{k}", "sresC"], wr=["sresC"])
        GD(lambda e: e.dma_start(out=yC[:], in_=y_d[k]), rd=[f"yd{k}", "yC"], wr=["yC"])
        for q in range(4):
            SY(lambda e, q=q: e.dma_start(out=xfC[:, q * H:(q + 1) * H],
                                          in_=x_d[3 + k * CH + q * 128:3 + k * CH + (q + 1) * 128, :]),
               rd=["xfC"], wr=["xfC"])
        def load_cb(n):
            bC = cbuf[n % 2]
            SY(lambda e, n=n, bC=bC: e.dma_start(
                out=bC[:], in_=bc_d[k, NST + n:NST + n + 1, :].partition_broadcast(128)),
               rd=[f"bcd{k}", f"cb{n % 2}"], wr=[f"cb{n % 2}"])
        load_cb(0)
        for n in range(NST):
            bC = cbuf[n % 2]
            if n + 1 < NST:
                load_cb(n + 1)
            for m in range(NDT):
                col = n * NDT + m
                j = (n * NDT + m) % 2
                ej = e2bf[j]
                ek = f"e2bf{j}"
                A_(lambda e, m=m, col=col, ej=ej: e.activation(
                    out=ej[:], in_=cumub[:, m * CH:(m + 1) * CH], func=AF.Exp,
                    scale=negA[:, col:col + 1], bias=lnh[:, col:col + 1]),
                   rd=["cumub", "negA", "lnh", ek], wr=[ek])
                V(lambda e, bC=bC, ej=ej: e.tensor_tensor(
                    out=ej[:], in0=ej[:], in1=bC[:], op=ALU.mult),
                  rd=[ek, f"cb{n % 2}"], wr=[ek])
                V(lambda e, m=m, col=col, ej=ej: e.scalar_tensor_tensor(
                    out=yC[:, m * CH:(m + 1) * CH], in0=ej[:], scalar=sgnh[:, col:col + 1],
                    in1=yC[:, m * CH:(m + 1) * CH], op0=ALU.mult, op1=ALU.add),
                  rd=[ek, "sgnh", "yC"], wr=["yC"])
        SY(lambda e: e.dma_start(out=cumub[:], in_=ub_d[k]), rd=[f"ubd{k}", "cumub"], wr=["cumub"])
        for m in range(NDT):
            V(lambda e, m=m: e.scalar_tensor_tensor(
                out=yC[:, m * CH:(m + 1) * CH], in0=cumub[:, m * CH:(m + 1) * CH],
                scalar=dl[:, m:m + 1], in1=yC[:, m * CH:(m + 1) * CH],
                op0=ALU.mult, op1=ALU.add), rd=["cumub", "dl", "yC"], wr=["yC"])
            V(lambda e, m=m: e.tensor_tensor(out=ygT[:, m * CH:(m + 1) * CH],
                                             in0=yC[:, m * CH:(m + 1) * CH],
                                             in1=sresC[:, m * CH:(m + 1) * CH], op=ALU.mult),
              rd=["yC", "sresC", "cumub"], wr=["cumub", f"yg{m}"])
        for ls in range(4):
            for hh in range(2):
                bank, bkey = pb[(ls * 2 + hh) % 4], f"PB{(ls * 2 + hh) % 4}"
                for m in range(NDT):
                    PE(lambda e, ls=ls, hh=hh, m=m, bank=bank: e.matmul(
                        bank[:], ygT[:, m * CH + ls * 128:m * CH + (ls + 1) * 128],
                        woutbf[:, m * H + hh * 512:m * H + (hh + 1) * 512],
                        start=(m == 0), stop=(m == NDT - 1)),
                       rd=[f"yg{m}", "woutsb"], wr=[bkey])
                V(lambda e, ls=ls, hh=hh, bank=bank: e.tensor_tensor(
                    out=xfC[:, ls * H + hh * 512:ls * H + (hh + 1) * 512],
                    in0=xfC[:, ls * H + hh * 512:ls * H + (hh + 1) * 512],
                    in1=bank[:], op=ALU.add), rd=[bkey, "xfC"], wr=["xfC"])
        for q in range(4):
            SY(lambda e, q=q: e.dma_start(out=x1_d[k * CH + q * 128:k * CH + (q + 1) * 128, :],
                                          in_=xfC[:, q * H:(q + 1) * H]),
               rd=["xfC"], wr=[f"x1d{k}"])
        for p in range(4):
            emit_ln(xfC[:, p * H:(p + 1) * H], xn2[:, p * H:(p + 1) * H], 128, "xfC", f"xn2_{p}", sq_scr2, c0=p * 8)
            for kk in range(NKH):
                PE(lambda e, p=p, kk=kk: e.transpose(
                    pbf[:, p * 128:(p + 1) * 128],
                    xn2[:, p * H + kk * 128:p * H + (kk + 1) * 128], identb[:]),
                   rd=[f"xn2_{p}", "identb"], wr=["PBF"])
                A_(lambda e, p=p, kk=kk: e.copy(
                    out=xn2T[:, kk * TOK + k * CH + p * 128:kk * TOK + k * CH + (p + 1) * 128],
                    in_=pbf[:, p * 128:(p + 1) * 128]),
                   rd=["PBF"], wr=[f"x2T{kk}"])

    for k in range(NCHUNK):
        P.phase = f"C{k}"
        phase_C(k)
    Creg.close()
    P.phase = "D"
    P.barrier()

    Dreg = ExitStack()

    def sbD(name, shape, dt):
        return Dreg.enter_context(nc.sbuf_tensor(name, shape, dt))

    scores = sbD("scores", [128, 16 * NEXP], F32)
    choice = sbD("choice", [128, 16 * NEXP], F32)
    msk = sbD("msk", [128, 16 * NEXP], F32)
    Wt = sbD("Wt", [128, 16 * NEXP], F32)
    eqs = sbD("eqs", [128, 16 * NEXP], F32)
    g1 = sbD("g1", [128, 16 * NGRP], F32)
    g2 = sbD("g2", [128, 16 * NGRP], F32)
    gs = sbD("gs", [128, 16 * NGRP], F32)
    thr = sbD("thr", [128, 16], F32)
    WT = sbD("WT", [32, TOK], F32)

    def r3(ap):
        return ap.rearrange("p (q e) -> p q e", e=NEXP)

    def r3g(ap):
        return ap.rearrange("p (q g) -> p q g", g=NGRP)

    def r4(ap):
        return ap.rearrange("p (q e) -> p q e", e=NEXP).rearrange(
            "p q (g j) -> p q g j", j=NEXP // NGRP)

    def bq(ap, last):
        return ap[:, 0:16].unsqueeze(2).broadcast_to((128, 16, last))

    def bqg(ap):
        return ap.rearrange("p (q g) -> p q g", g=NGRP).unsqueeze(3).broadcast_to((128, 16, NGRP, NEXP // NGRP))

    for q in range(16):
        bank, bkey = pb[q % 2], f"PB{q % 2}"
        for kk in range(NKH):
            PE(lambda e, q=q, kk=kk, bank=bank: e.matmul(
                bank[:, 0:NEXP], xn2T[:, kk * TOK + q * 128:kk * TOK + (q + 1) * 128],
                gwTbf[:, kk * NEXP:(kk + 1) * NEXP], start=(kk == 0), stop=(kk == NKH - 1)),
               rd=[f"x2T{kk}", "gwTbf"], wr=[bkey])
        A_(lambda e, q=q, bank=bank: e.activation(
            out=scores[:, q * NEXP:(q + 1) * NEXP], in_=bank[:, 0:NEXP], func=AF.Sigmoid),
           rd=[bkey], wr=["scores"])
    V(lambda e: e.tensor_tensor(out=r3(choice), in0=r3(scores),
                                in1=gbb[:].unsqueeze(1).broadcast_to((128, 16, NEXP)),
                                op=ALU.add), rd=["scores", "gbb"], wr=["choice"])
    V(lambda e: e.tensor_reduce(out=r3g(g1), in_=r4(choice), axis=AX, op=ALU.max),
      rd=["choice"], wr=["g1"])
    V(lambda e: e.tensor_tensor(out=r4(eqs), in0=r4(choice), in1=bqg(g1), op=ALU.is_ge),
      rd=["choice", "g1"], wr=["eqs"])
    V(lambda e: e.scalar_tensor_tensor(out=r4(msk), in0=r4(eqs), scalar=-1e30, in1=r4(choice),
                                       op0=ALU.mult, op1=ALU.add), rd=["eqs", "choice"], wr=["msk"])
    V(lambda e: e.tensor_reduce(out=r3g(g2), in_=r4(msk), axis=AX, op=ALU.max),
      rd=["msk"], wr=["g2"])
    V(lambda e: e.tensor_tensor(out=r3g(gs), in0=r3g(g1), in1=r3g(g2), op=ALU.add),
      rd=["g1", "g2"], wr=["gs"])
    V(lambda e: e.tensor_reduce(out=thr[:, 0:16], in_=r3g(gs), axis=AX, op=ALU.max),
      rd=["gs"], wr=["thr"])
    V(lambda e: e.tensor_tensor(out=r3g(g1), in0=r3g(gs), in1=bq(thr, NGRP), op=ALU.is_ge),
      rd=["gs", "thr"], wr=["g1"])
    V(lambda e: e.scalar_tensor_tensor(out=r3g(g2), in0=r3g(g1), scalar=-1e30, in1=r3g(gs),
                                       op0=ALU.mult, op1=ALU.add), rd=["g1", "gs"], wr=["g2"])
    V(lambda e: e.tensor_reduce(out=thr[:, 0:16], in_=r3g(g2), axis=AX, op=ALU.max),
      rd=["g2"], wr=["thr"])
    V(lambda e: e.tensor_tensor(out=r3g(g1), in0=r3g(gs), in1=bq(thr, NGRP), op=ALU.is_ge),
      rd=["gs", "thr"], wr=["g1"])
    V(lambda e: e.tensor_tensor(out=r4(msk), in0=r4(choice), in1=bqg(g1), op=ALU.mult),
      rd=["choice", "g1"], wr=["msk"])
    for it in range(TOPK):
        V(lambda e: e.tensor_reduce(out=thr[:, 0:16], in_=r3(msk), axis=AX, op=ALU.max),
          rd=["msk"], wr=["thr"])
        if it < TOPK - 1:
            V(lambda e: e.tensor_tensor(out=r3(eqs), in0=r3(msk), in1=bq(thr, NEXP), op=ALU.is_ge),
              rd=["msk", "thr"], wr=["eqs"])
            V(lambda e: e.scalar_tensor_tensor(out=r3(msk), in0=r3(eqs), scalar=-1e30, in1=r3(msk),
                                               op0=ALU.mult, op1=ALU.add), rd=["eqs", "msk"], wr=["msk"])
    V(lambda e: e.tensor_tensor(out=r4(eqs), in0=r4(choice), in1=bqg(g1), op=ALU.mult),
      rd=["choice", "g1"], wr=["eqs"])
    V(lambda e: e.tensor_tensor(out=r3(eqs), in0=r3(eqs), in1=bq(thr, NEXP), op=ALU.is_ge),
      rd=["eqs", "thr"], wr=["eqs"])
    V(lambda e: e.tensor_tensor(out=r3(Wt), in0=r3(scores), in1=r3(eqs), op=ALU.mult),
      rd=["scores", "eqs"], wr=["Wt"])
    V(lambda e: e.tensor_reduce(out=thr[:, 0:16], in_=r3(Wt), axis=AX, op=ALU.add),
      rd=["Wt"], wr=["thr"])
    V(lambda e: e.tensor_scalar(out=thr[:, 0:16], in0=thr[:, 0:16], scalar1=1e-20, scalar2=None,
                                op0=ALU.add), rd=["thr"], wr=["thr"])
    V(lambda e: e.reciprocal(out=thr[:, 0:16], in_=thr[:, 0:16]), rd=["thr"], wr=["thr"])
    V(lambda e: e.scalar_tensor_tensor(out=r3(Wt), in0=r3(Wt), scalar=RSCALE, in1=bq(thr, NEXP),
                                       op0=ALU.mult, op1=ALU.mult), rd=["Wt", "thr"], wr=["Wt"])
    for q in range(16):
        PE(lambda e, q=q: e.transpose(pb[6][0:NEXP, 0:128], Wt[:, q * NEXP:(q + 1) * NEXP],
                                      identf[:]), rd=["Wt", "identf"], wr=["PB6"])
        A_(lambda e, q=q: e.copy(out=WT[:, q * 128:(q + 1) * 128], in_=pb[6][0:NEXP, 0:128]),
           rd=["PB6"], wr=["WT"])
    GD(lambda e: e.dma_start(out=wt_d[:], in_=WT[:]), rd=["WT"], wr=["wtd"])

    Dreg.close()
    P.barrier()

    xf2 = [sbG(f"xf2_{i}", [128, H], F32) for i in range(2)]
    Ereg = ExitStack()

    def sbE(name, shape, dt):
        return Ereg.enter_context(nc.sbuf_tensor(name, shape, dt))

    slot = [dict(eg=sbE(f"eg{i}", [128, NKH * 512], BF16),
                 eu=sbE(f"eu{i}", [128, NKH * 512], BF16),
                 ed=sbE(f"ed{i}", [128, 4 * H], BF16)) for i in range(2)]
    acc = [sbE(f"acc{q}", [128, H], F32) for q in range(16)]
    tts = [sbE(f"tt{i}", [128, 4 * CH], BF16) for i in range(2)]
    Wb = [sbE(f"Wb{i}", [128, TOK], BF16) for i in range(2)]
    sg_scr = esA[0]
    ts_e = tsA[0]

    def load_mat(si, eidx, which):
        s = slot[si]
        key = f"{which}{si}"
        if eidx < NEXP:
            dram = {"eg": egT_d, "eu": euT_d, "ed": edT_d}[which]
            if which == "ed":
                GD(lambda e: e.dma_start(out=s["ed"][:].rearrange("p (k i) -> p k i", i=H),
                                         in_=dram[eidx].rearrange("(k p) i -> p k i", p=128)),
                   rd=[key], wr=[key])
            else:
                GD(lambda e, dram=dram: e.dma_start(out=s[which][:].rearrange("p (k i) -> p k i", i=512),
                                                    in_=dram[eidx].rearrange("(k p) i -> p k i", p=128)),
                   rd=[key], wr=[key])
        else:
            hb = (eidx - NEXP) * 512
            if which == "eg":
                GD(lambda e: e.dma_start(out=s["eg"][:].rearrange("p (k i) -> p k i", i=512),
                                         in_=sgT_d[:, hb:hb + 512].rearrange("(k p) i -> p k i", p=128)),
                   rd=[key], wr=[key])
            elif which == "eu":
                GD(lambda e: e.dma_start(out=s["eu"][:].rearrange("p (k i) -> p k i", i=512),
                                         in_=suT_d[:, hb:hb + 512].rearrange("(k p) i -> p k i", p=128)),
                   rd=[key], wr=[key])
            else:
                GD(lambda e: e.dma_start(out=s["ed"][:].rearrange("p (k i) -> p k i", i=H),
                                         in_=sdT_d[hb:hb + 512, :].rearrange("(k p) i -> p k i", p=128)),
                   rd=[key], wr=[key])

    def load_Wb(si, eidx):
        if eidx < NEXP:
            SY(lambda e: e.dma_start(out=Wb[si][:], in_=wt_d[eidx:eidx + 1, :].partition_broadcast(128)),
               rd=["wtd", f"Wbb{si}"], wr=[f"Wbb{si}"])

    def expert_gu(si, lc, routed):
        s = slot[si]
        lo = lc * CH
        for im in range(4):
            bank, bkey = pb[im % 2], f"PB{im % 2}"
            for kk in range(NKH):
                PE(lambda e, im=im, kk=kk, bank=bank: e.matmul(
                    bank[:], s["eg"][:, kk * 512 + im * 128:kk * 512 + (im + 1) * 128],
                    xn2T[:, kk * TOK + lo:kk * TOK + lo + CH],
                    start=(kk == 0), stop=(kk == NKH - 1)),
                   rd=[f"eg{si}", f"x2T{kk}"], wr=[bkey])
            bu_, buk = pb[2 + im], f"PB{2 + im}"
            for kk in range(NKH):
                PE(lambda e, im=im, kk=kk, bu_=bu_: e.matmul(
                    bu_[:], s["eu"][:, kk * 512 + im * 128:kk * 512 + (im + 1) * 128],
                    xn2T[:, kk * TOK + lo:kk * TOK + lo + CH],
                    start=(kk == 0), stop=(kk == NKH - 1)),
                   rd=[f"eu{si}", f"x2T{kk}"], wr=[buk])
            A_(lambda e, im=im, bank=bank: e.activation(out=sg_scr[:], in_=bank[:], func=AF.Silu),
               rd=[bkey, "esA0"], wr=["esA0"])
            V(lambda e, im=im, bu_=bu_: e.tensor_tensor(out=ts_e[:], in0=sg_scr[:], in1=bu_[:],
                                                        op=ALU.mult),
              rd=["esA0", buk, "tsA0"], wr=["tsA0"])
            if routed:
                V(lambda e, im=im, lo=lo: e.tensor_tensor(
                    out=tts[si][:, im * CH:(im + 1) * CH], in0=ts_e[:],
                    in1=Wb[si][:, lo:lo + CH], op=ALU.mult),
                  rd=["tsA0", f"Wbb{si}"], wr=[f"tt{si}_{im}"])
            else:
                V(lambda e, im=im: e.tensor_copy(out=tts[si][:, im * CH:(im + 1) * CH], in_=ts_e[:]),
                  rd=["tsA0"], wr=[f"tt{si}_{im}"])

    def down_pair(lc, first_pair):
        for ls in range(4):
            q = lc * 4 + ls
            for hh in range(2):
                gi = (ls * 2 + hh) % 2
                bank, bkey = (pb[6], "PB6") if gi == 0 else (pb[0], "PB0")
                mms = [(si, im) for si in (0, 1) for im in range(4)]
                for j, (si, im) in enumerate(mms):
                    PE(lambda e, si=si, im=im, ls=ls, hh=hh, bank=bank,
                       st=(j == 0), sp=(j == len(mms) - 1): e.matmul(
                        bank[:], tts[si][:, im * CH + ls * 128:im * CH + (ls + 1) * 128],
                        slot[si]["ed"][:, im * H + hh * 512:im * H + (hh + 1) * 512],
                        start=st, stop=sp),
                       rd=[f"tt{si}_{im}", f"ed{si}"], wr=[bkey])
                if first_pair:
                    V(lambda e, q=q, hh=hh, bank=bank: e.tensor_copy(
                        out=acc[q][:, hh * 512:(hh + 1) * 512], in_=bank[:]),
                      rd=[bkey, f"acc{q}_{hh}"], wr=[f"acc{q}_{hh}"])
                else:
                    V(lambda e, q=q, hh=hh, bank=bank: e.tensor_tensor(
                        out=acc[q][:, hh * 512:(hh + 1) * 512],
                        in0=acc[q][:, hh * 512:(hh + 1) * 512], in1=bank[:], op=ALU.add),
                      rd=[bkey, f"acc{q}_{hh}"], wr=[f"acc{q}_{hh}"])

    def emit_pair_loads(pair, mats):
        for si in (0, 1):
            eidx = 2 * pair + si
            if eidx >= 34:
                return
            for w in mats:
                load_mat(si, eidx, w)
            if "eg" in mats:
                load_Wb(si, eidx)

    emit_pair_loads(0, ("eg", "eu", "ed"))
    for pair in range(17):
        for lc in range(NCHUNK):
            last = lc == NCHUNK - 1
            expert_gu(0, lc, routed=(2 * pair < NEXP))
            if last:
                for si0 in (0,):
                    if 2 * (pair + 1) < 34:
                        load_mat(0, 2 * (pair + 1), "eg")
                        load_mat(0, 2 * (pair + 1), "eu")
                        load_Wb(0, 2 * (pair + 1))
            expert_gu(1, lc, routed=(2 * pair + 1 < NEXP))
            if last:
                if 2 * (pair + 1) + 1 < 34:
                    load_mat(1, 2 * (pair + 1) + 1, "eg")
                    load_mat(1, 2 * (pair + 1) + 1, "eu")
                    load_Wb(1, 2 * (pair + 1) + 1)
            down_pair(lc, first_pair=(pair == 0))
            if last:
                if 2 * (pair + 1) < 34:
                    load_mat(0, 2 * (pair + 1), "ed")
                if 2 * (pair + 1) + 1 < 34:
                    load_mat(1, 2 * (pair + 1) + 1, "ed")

    P.phase = "F"
    for q in range(16):
        j = q % 2
        xb, xk = xf2[j], f"xf2_{j}"
        SY(lambda e, q=q, xb=xb: e.dma_start(out=xb[:, 0:H], in_=x1_d[q * 128:(q + 1) * 128, :]),
           rd=[f"x1d{q // 4}", xk], wr=[xk])
        for hh in range(2):
            V(lambda e, q=q, hh=hh, xb=xb: e.tensor_tensor(
                out=acc[q][:, hh * 512:(hh + 1) * 512], in0=xb[:, hh * 512:(hh + 1) * 512],
                in1=acc[q][:, hh * 512:(hh + 1) * 512], op=ALU.add),
              rd=[xk, f"acc{q}_{hh}"], wr=[f"acc{q}_{hh}"])
        SY(lambda e, q=q: e.dma_start(out=out_d[q * 128:(q + 1) * 128, :], in_=acc[q][:, 0:H]),
           rd=[f"acc{q}_0", f"acc{q}_1"], wr=[f"out{q}"])
    P.emit("dve", lambda e: e.nop(), rd=[f"out{q}" for q in range(16)], wr=[])

    Ereg.close()

    sems = {}
    sem_es = ExitStack()
    sem_names = ["pe", "act", "dve", "pool"] + \
        [f"sync{i}" for i in range(Prog.NSLOT)] + [f"gdma{i}" for i in range(Prog.NSLOT)]
    for s in sem_names:
        sems[s] = sem_es.enter_context(nc.semaphore(f"sem_{s}"))
    run = P.materialize(sems)
    with nc.Block() as block:
        @block.tensor
        def _(eng):
            run(("pe",), eng, nc)

        @block.scalar
        def _(eng):
            run(("act",), eng, nc)

        @block.vector
        def _(eng):
            run(("dve",), eng, nc)

        @block.gpsimd
        def _(eng):
            run(("gdma", "pool"), eng, nc)

        @block.sync
        def _(eng):
            run(("sync",), eng, nc)
    ES.close()
    sem_es.close()
    nc.finalize()
    return nc


def prep_inputs(inputs):
    x = np.asarray(inputs["L0_attn_x"], np.float32)
    shared = {}
    shared["w_in_f"] = np.ascontiguousarray(inputs["L0_attn_w_in"], dtype=np.float32)
    shared["b_in_l"] = np.ascontiguousarray(np.asarray(inputs["L0_attn_b_in"], np.float32).reshape(32, 128).T)
    wc = np.asarray(inputs["L0_attn_w_conv"], np.float32)
    shared["wconv_l"] = np.ascontiguousarray(wc.reshape(TW, NDT, 128).transpose(2, 1, 0).reshape(128, NDT * TW))
    shared["bconv_l"] = np.ascontiguousarray(np.asarray(inputs["L0_attn_b_conv"], np.float32).reshape(NDT, 128).T)
    shared["wx_f"] = np.ascontiguousarray(inputs["L0_attn_w_x"], dtype=np.float32)
    shared["wdt_f"] = np.ascontiguousarray(inputs["L0_attn_w_dt"], dtype=np.float32)
    al = np.asarray(inputs["L0_attn_A_log"], np.float32)
    shared["alog_l"] = np.ascontiguousarray(al.reshape(NDT, 128, NST).transpose(1, 2, 0).reshape(128, NST * NDT))
    shared["d_l"] = np.ascontiguousarray(np.asarray(inputs["L0_attn_D"], np.float32).reshape(NDT, 128).T)
    shared["wout_f"] = np.ascontiguousarray(inputs["L0_attn_w_out"], dtype=np.float32)
    shared["gwT_f"] = np.ascontiguousarray(np.asarray(inputs["L0_ffn_gate_w"], np.float32).T)
    shared["gb_row"] = np.ascontiguousarray(np.asarray(inputs["L0_ffn_gate_bias"], np.float32).reshape(1, NEXP))
    shared["egT_f"] = np.ascontiguousarray(np.asarray(inputs["L0_ffn_expert_gate"], np.float32).transpose(0, 2, 1))
    shared["euT_f"] = np.ascontiguousarray(np.asarray(inputs["L0_ffn_expert_up"], np.float32).transpose(0, 2, 1))
    shared["edT_f"] = np.ascontiguousarray(np.asarray(inputs["L0_ffn_expert_down"], np.float32).transpose(0, 2, 1))
    shared["sgT_f"] = np.ascontiguousarray(np.asarray(inputs["L0_ffn_shared_gate"], np.float32).T)
    shared["suT_f"] = np.ascontiguousarray(np.asarray(inputs["L0_ffn_shared_up"], np.float32).T)
    shared["sdT_f"] = np.ascontiguousarray(np.asarray(inputs["L0_ffn_shared_down"], np.float32).T)
    shared["ident_bf"] = np.eye(128, dtype=ml_dtypes.bfloat16)
    shared["ident_f"] = np.eye(128, dtype=np.float32)

    in_maps = []
    for c in range(NCORE):
        b, pos = c // 4, c % 4
        lo = pos * TOK
        if pos == 0:
            xs = np.concatenate([np.zeros((3, H), np.float32), x[b, lo:lo + TOK]], axis=0)
        else:
            xs = x[b, lo - 3:lo + TOK]
        m = dict(shared)
        m["x_shard"] = np.ascontiguousarray(xs)
        m["mask_in"] = np.array([[0.0 if pos == 0 else 1.0]], np.float32)
        pm = np.zeros((4, 1), np.float32)
        pm[pos, 0] = 1.0
        m["posmask"] = pm
        in_maps.append(m)
    return in_maps


def kernel(**inputs):
    in_maps = prep_inputs(inputs)
    nc = make_program(do_cc=True)
    res = run_bass_kernel_spmd(nc, in_maps, core_ids=list(range(NCORE)))
    out = np.empty((Bb, Lb, H), np.float32)
    for c in range(NCORE):
        b, pos = c // 4, c % 4
        out[b, pos * TOK:(pos + 1) * TOK, :] = res.results[c]["out"]
    return out
